# version 05 (final)
# baseline (speedup 1.0000x reference)
import numpy as np
from contextlib import ExitStack
import concourse.bass as bass
import concourse.mybir as mybir
from concourse.bass_utils import run_bass_kernel_spmd

F32 = mybir.dt.float32
BF16 = mybir.dt.bfloat16
AF = mybir.ActivationFunctionType
ALU = mybir.AluOpType
AX = mybir.AxisListType

NCORES = 8
D = 1024
B, S = 4, 4096
NTOK = B * S
TPC = NTOK // NCORES



class Buf:
    __slots__ = ("w", "r")
    def __init__(self):
        self.w = None
        self.r = []


class Prog:

    ENGINES = ("tensor", "vector", "scalar", "sync", "gpsimd")

    def __init__(self, nc):
        self.nc = nc
        self.queues = {e: [] for e in self.ENGINES}
        self.cnt = {e: 0 for e in self.ENGINES}
        self.floor = {}

    def barrier(self):
        self.floor = dict(self.cnt)

    def op(self, eng, fn, ins=(), outs=(), dma=False):
        deps = {}
        def add(tok):
            if tok is None:
                return
            e, v = tok
            if deps.get(e, -1) < v:
                deps[e] = v
        for e, v in self.floor.items():
            if v > 0:
                add((e, v))
        for b in ins:
            add(b.w)
        for b in outs:
            add(b.w)
            for t in b.r:
                add(t)
        inc = 16 if dma else 1
        self.cnt[eng] += inc
        tok = (eng, self.cnt[eng])
        self.queues[eng].append((dict(deps), fn, inc))
        for b in ins:
            b.r.append(tok)
        for b in outs:
            b.w = tok
            b.r = []
        return tok

    def emit(self, es):
        sems = {e: es.enter_context(self.nc.semaphore("sem_" + e))
                for e in self.ENGINES}
        block = es.enter_context(self.nc.Block())
        waited = {e: {e2: 0 for e2 in self.ENGINES} for e in self.ENGINES}

        def emit_queue(q, eng):
            wtd = waited[eng]
            for deps, fn, inc in self.queues[eng]:
                for e, v in deps.items():
                    if wtd[e] < v:
                        q.wait_ge(sems[e], v)
                        wtd[e] = v
                instr = fn(q)
                instr.then_inc(sems[eng], inc)
            if eng == "gpsimd":
                q.wait_ge(sems["gpsimd"], self.cnt["gpsimd"])

        @block.tensor
        def _(q):
            emit_queue(q, "tensor")

        @block.vector
        def _(q):
            emit_queue(q, "vector")

        @block.scalar
        def _(q):
            emit_queue(q, "scalar")

        @block.sync
        def _(q):
            emit_queue(q, "sync")

        @block.gpsimd
        def _(q):
            emit_queue(q, "gpsimd")


class Ctx:

    def __init__(self, name):
        self.nc = bass.Bass("TRN2", target_bir_lowering=False)
        self.p = Prog(self.nc)
        self.es = ExitStack()
        self.bufs = {}

    def dram(self, name, shape, dt=F32, kind="ExternalInput"):
        h = self.nc.dram_tensor(name, list(shape), dt, kind=kind)
        ap = h.ap()
        self.bufs[id(ap)] = Buf()
        return ap

    def sb(self, name, shape, dt=F32):
        cm = self.nc.sbuf_tensor(name, list(shape), dt)
        h = cm.__enter__()
        self.cms = getattr(self, "cms", {})
        self.cms[id(h)] = cm
        self.bufs[id(h)] = Buf()
        return h

    def free(self, *hs):
        for h in hs:
            self.cms.pop(id(h)).__exit__(None, None, None)

    def barrier(self):
        self.p.barrier()

    def ps(self, name, shape, dt=F32):
        cm = self.nc.psum_tensor(name, list(shape), dt)
        h = cm.__enter__()
        self.cms = getattr(self, "cms", {})
        self.cms[id(h)] = cm
        self.bufs[id(h)] = Buf()
        return h

    def B(self, h):
        return self.bufs[id(h)]

    def op(self, eng, fn, ins=(), outs=(), dma=False):
        return self.p.op(eng, fn,
                         ins=[self.B(h) for h in ins],
                         outs=[self.B(h) for h in outs], dma=dma)

    def finish(self):
        self.p.emit(self.es)
        self.es.close()
        for cm in reversed(list(getattr(self, "cms", {}).values())):
            cm.__exit__(None, None, None)
        self.cms = {}
        return self.nc


def const_inputs():
    return {
        "c_ident": np.eye(128, dtype=np.float32),
        "c_identb": np.eye(128, dtype=np.float32),
        "c_ones": np.ones((1, 512), dtype=np.float32),
        "c_iota": np.broadcast_to(np.arange(512, dtype=np.float32), (128, 512)).copy(),
    }


def decl_consts(c, iota=False, onesb=False, ones128=False):
    c.ident = c.dram("c_ident", [128, 128])
    c.ones = c.dram("c_ones", [1, 512])
    c.iota = c.dram("c_iota", [128, 512])
    c.s_ident = c.sb("s_ident", [128, 128], F32)
    c.s_ones = c.sb("s_ones", [1, 512], F32)
    c.op("sync", lambda q: q.dma_start(out=c.s_ident[:, :], in_=c.ident[:, :]),
         outs=[c.s_ident], dma=True)
    c.op("sync", lambda q: q.dma_start(out=c.s_ones[:, :], in_=c.ones[:, :]),
         outs=[c.s_ones], dma=True)
    if iota:
        c.s_iota = c.sb("s_iota", [128, 512], F32)
        c.op("sync", lambda q: q.dma_start(out=c.s_iota[:, :], in_=c.iota[:, :]),
             outs=[c.s_iota], dma=True)
    if onesb:
        c.s_onesb = c.sb("s_onesb", [1, 512], BF16)
        c.op("scalar", lambda q: q.copy(out=c.s_onesb[:, :], in_=c.s_ones[:, :]),
             ins=[c.s_ones], outs=[c.s_onesb])
    if ones128:
        c.s_ones128 = c.sb("s_ones128", [128, 128], BF16)
        pb = c.ps("p_ones128", [128, 128], F32)
        c.p_ones128 = pb
        c.op("tensor", lambda q: q.matmul(out=pb[:, :], lhsT=c.s_ones[0:1, 0:128],
             rhs=c.s_ones[0:1, 0:128], start=True, stop=True),
             ins=[c.s_ones], outs=[pb])
        c.op("vector", lambda q: q.tensor_copy(out=c.s_ones128[:, :], in_=pb[:, :]),
             ins=[pb], outs=[c.s_ones128])


def layernorm_tile(c, x_ap, out_ap, x_h, out_h, stat, sq, eps=1e-5, width=D,
                   rms=False):
    st = stat
    if not rms:
        c.op("vector", lambda q: q.reduce_sum(out=st[:, 0:1], in_=x_ap, axis=AX.X),
             ins=[x_h], outs=[st])
        c.op("vector", lambda q: q.tensor_scalar(out=st[:, 1:2], in0=st[:, 0:1],
             scalar1=1.0 / width, scalar2=None, op0=ALU.mult), ins=[st], outs=[st])
        c.op("vector", lambda q: q.tensor_scalar(out=out_ap, in0=x_ap,
             scalar1=st[:, 1:2], scalar2=None, op0=ALU.subtract),
             ins=[x_h, st], outs=[out_h])
        src_ap, src_h = out_ap, out_h
    else:
        src_ap, src_h = x_ap, x_h
    c.op("scalar", lambda q: q.activation(out=sq[:, 0:width], in_=src_ap,
         func=AF.Square), ins=[src_h], outs=[sq])
    c.op("vector", lambda q: q.reduce_sum(out=st[:, 2:3], in_=sq[:, 0:width],
         axis=AX.X), ins=[sq], outs=[st])
    c.op("vector", lambda q: q.tensor_scalar(out=st[:, 3:4], in0=st[:, 2:3],
         scalar1=1.0 / width, scalar2=eps, op0=ALU.mult, op1=ALU.add),
         ins=[st], outs=[st])
    c.op("scalar", lambda q: q.activation(out=st[:, 3:4], in_=st[:, 3:4],
         func=AF.Sqrt), ins=[st], outs=[st])
    c.op("vector", lambda q: q.reciprocal(out=st[:, 3:4], in_=st[:, 3:4]),
         ins=[st], outs=[st])
    c.op("vector", lambda q: q.tensor_scalar(out=out_ap, in0=src_ap,
         scalar1=st[:, 3:4], scalar2=None, op0=ALU.mult),
         ins=[st, src_h], outs=[out_h])


def row_bcast(c, row_ap, row_h, out_ap, out_h, psum, n):
    c.op("tensor", lambda q: q.matmul(out=psum[:, 0:n], lhsT=c.s_ones[0:1, 0:128],
         rhs=row_ap, start=True, stop=True), ins=[c.s_ones, row_h], outs=[psum])
    c.op("vector", lambda q: q.tensor_copy(out=out_ap, in_=psum[:, 0:n]),
         ins=[psum], outs=[out_h])


def col_load(c, vec_dram, L, name, psum, rowbuf):
    nb = L // 128
    out = c.sb(name, [128, nb], F32)
    c.op("sync", lambda q: q.dma_start(out=rowbuf[0:1, 0:L], in_=vec_dram),
         outs=[rowbuf], dma=True)
    def mk(i):
        return lambda q: q.matmul(out=psum[:, i:i + 1], lhsT=rowbuf[0:1, i * 128:(i + 1) * 128],
                                  rhs=c.s_ones[0:1, 0:1], start=True, stop=True)
    for i in range(nb):
        c.op("tensor", mk(i), ins=[rowbuf, c.s_ones], outs=[psum])
    c.op("vector", lambda q: q.tensor_copy(out=out[:, :], in_=psum[:, 0:nb]),
         ins=[psum], outs=[out])
    return out



def build_l1ffn():
    c = Ctx("l1ffn")
    x_h = c.dram("x", [TPC, D])
    y_h = c.dram("y", [TPC, D], kind="ExternalOutput")
    w1T_h = c.dram("w1T", [1024, 4096])
    w2T_h = c.dram("w2T", [1024, 4096])
    h1T_h = c.dram("h1T", [4096, 4096])
    h2T_h = c.dram("h2T", [4096, 4096])
    w3T_h = c.dram("w3T", [4096, 1024])
    s1_h = c.dram("s1", [1, 4096])
    s2_h = c.dram("s2", [1, 4096])
    sh1_h = c.dram("sh1", [1, 4096])
    sh2_h = c.dram("sh2", [1, 4096])
    s3_h = c.dram("s3", [1, 1024])
    decl_consts(c)

    ps = [c.ps("p%d" % i, [128, 512], F32) for i in range(8)]

    sx = c.sb("sx", [128, 4 * 1024], F32)
    xn = c.sb("xn", [128, 1024], F32)
    sq = c.sb("sq", [128, 1024], F32)
    stat = c.sb("stat", [128, 4], F32)
    Xt = c.sb("Xt", [128, 8 * 512], BF16)
    wstage = c.sb("wstage", [128, 4096], F32)
    wb = [c.sb("wb%d" % i, [128, 4096], BF16) for i in range(2)]
    hm = c.sb("hm", [128, 32 * 512], BF16)
    g = c.sb("g", [128, 32 * 512], BF16)
    tmpf = c.sb("tmpf", [128, 512], F32)
    ybuf = c.sb("ybuf", [128, 1024], F32)

    rowbuf = c.sb("rowbuf", [1, 4096], F32)
    s1c = col_load(c, s1_h[0, :], 4096, "s1c", ps[0], rowbuf)
    s2c = col_load(c, s2_h[0, :], 4096, "s2c", ps[0], rowbuf)
    sh1c = col_load(c, sh1_h[0, :], 4096, "sh1c", ps[0], rowbuf)
    sh2c = col_load(c, sh2_h[0, :], 4096, "sh2c", ps[0], rowbuf)
    s3bc = c.sb("s3bc", [128, 1024], F32)
    c.op("sync", lambda q: q.dma_start(out=rowbuf[0:1, 0:1024], in_=s3_h[0, :]),
         outs=[rowbuf], dma=True)
    for i in range(2):
        row_bcast(c, rowbuf[0:1, i * 512:(i + 1) * 512], rowbuf,
                  s3bc[:, i * 512:(i + 1) * 512], s3bc, ps[0], 512)

    XtV = Xt.rearrange("p (dc t) -> p dc t", t=512)

    def wload(dst_off, src_h, src_rows, col0, ncols, nchunks_per_dma=4):
        nch = src_rows // 128
        v = src_h.rearrange("(c p) m -> p c m", p=128)
        per = nchunks_per_dma
        for i0 in range(0, nch, per):
            n = min(per, nch - i0)
            o = dst_off + i0 * ncols
            out_ap = wstage[:, o:o + n * ncols].rearrange(
                "p (c m) -> p c m", m=ncols)
            in_ap = v[:, i0:i0 + n, col0:col0 + ncols]
            c.op("sync", lambda q, a=out_ap, b=in_ap: q.dma_start(out=a, in_=b),
                 outs=[wstage], dma=True)

    for tc in range(4):
        for tt in range(4):
            c.op("sync", lambda q, tc=tc, tt=tt: q.dma_start(
                out=sx[:, tt * 1024:(tt + 1) * 1024],
                in_=x_h[tc * 512 + tt * 128: tc * 512 + (tt + 1) * 128, :]),
                outs=[sx], dma=True)
            layernorm_tile(c, sx[:, tt * 1024:(tt + 1) * 1024], xn[:, 0:1024],
                           sx, xn, stat, sq)
            for dgrp in range(2):
                def mk_tr(q, dgrp=dgrp):
                    for k in range(4):
                        r = q.transpose(out=ps[0][:, k * 128:(k + 1) * 128],
                                        in_=xn[:, (4 * dgrp + k) * 128:(4 * dgrp + k + 1) * 128],
                                        identity=c.s_ident[:, :])
                    return r
                c.op("tensor", mk_tr, ins=[xn, c.s_ident], outs=[ps[0]])
                c.op("scalar", lambda q, dgrp=dgrp, tt=tt: q.copy(
                    out=XtV[:, 4 * dgrp:4 * dgrp + 4, tt * 128:(tt + 1) * 128],
                    in_=ps[0][:, 0:512].rearrange("p (c t) -> p c t", t=128)),
                    ins=[ps[0]], outs=[Xt])

        for mt in range(32):
            wload(0, w1T_h, 1024, mt * 128, 128)
            wload(1024, w2T_h, 1024, mt * 128, 128)
            wbc = wb[mt % 2]
            c.op("scalar", lambda q, wbc=wbc: q.copy(out=wbc[:, 0:2048],
                 in_=wstage[:, 0:2048]), ins=[wstage], outs=[wbc])
            psA = ps[1 + 2 * (mt % 2)]
            psB = ps[2 + 2 * (mt % 2)]
            def mmA(q, wbc=wbc):
                for dc in range(8):
                    r = q.matmul(out=psA[:, :], lhsT=wbc[:, dc * 128:(dc + 1) * 128],
                                 rhs=Xt[:, dc * 512:(dc + 1) * 512],
                                 start=(dc == 0), stop=(dc == 7))
                return r
            def mmB(q, wbc=wbc):
                for dc in range(8):
                    r = q.matmul(out=psB[:, :], lhsT=wbc[:, 1024 + dc * 128:1024 + (dc + 1) * 128],
                                 rhs=Xt[:, dc * 512:(dc + 1) * 512],
                                 start=(dc == 0), stop=(dc == 7))
                return r
            c.op("tensor", mmA, ins=[wbc, Xt], outs=[psA])
            c.op("tensor", mmB, ins=[wbc, Xt], outs=[psB])
            c.op("scalar", lambda q, psA=psA, mt=mt: q.activation(
                out=tmpf[:, :], in_=psA[:, :], func=AF.Silu,
                scale=s1c[:, mt:mt + 1]), ins=[psA, s1c], outs=[tmpf])
            c.op("vector", lambda q, psB=psB, mt=mt: q.scalar_tensor_tensor(
                out=hm[:, mt * 512:(mt + 1) * 512], in0=psB[:, :],
                scalar=s2c[:, mt:mt + 1], in1=tmpf[:, :],
                op0=ALU.mult, op1=ALU.mult), ins=[psB, s2c, tmpf], outs=[hm])

        for (srcb, dstb, wT_h, sc) in ((hm, g, h1T_h, sh1c), (g, hm, h2T_h, sh2c)):
            for mt in range(32):
                wload(0, wT_h, 4096, mt * 128, 128)
                wbc = wb[mt % 2]
                c.op("scalar", lambda q, wbc=wbc: q.copy(out=wbc[:, 0:4096],
                     in_=wstage[:, 0:4096]), ins=[wstage], outs=[wbc])
                psA = ps[1 + (mt % 2)]
                def mmC(q, wbc=wbc, srcb=srcb):
                    for dc in range(32):
                        r = q.matmul(out=psA[:, :], lhsT=wbc[:, dc * 128:(dc + 1) * 128],
                                     rhs=srcb[:, dc * 512:(dc + 1) * 512],
                                     start=(dc == 0), stop=(dc == 31))
                    return r
                c.op("tensor", mmC, ins=[wbc, srcb], outs=[psA])
                c.op("scalar", lambda q, psA=psA, mt=mt, dstb=dstb, sc=sc: q.activation(
                    out=dstb[:, mt * 512:(mt + 1) * 512], in_=psA[:, :],
                    func=AF.Gelu, scale=sc[:, mt:mt + 1]),
                    ins=[psA, sc], outs=[dstb])

        for half in range(2):
            for pc in range(8):
                v = w3T_h.rearrange("(c p) m -> p c m", p=128)
                c0 = half * 16 + pc * 2
                c.op("sync", lambda q, c0=c0: q.dma_start(
                    out=wstage[:, 0:2048].rearrange("p (c m) -> p c m", m=1024),
                    in_=v[:, c0:c0 + 2, :]), outs=[wstage], dma=True)
                c.op("scalar", lambda q, pc=pc: q.copy(
                    out=g[:, pc * 2048:(pc + 1) * 2048], in_=wstage[:, 0:2048]),
                    ins=[wstage], outs=[g])
            def mmE(q, half=half):
                for hd_c in range(16):
                    for tt in range(4):
                        for n2 in range(2):
                            r = q.matmul(
                                out=ps[tt * 2 + n2][:, :],
                                lhsT=hm[:, (half * 16 + hd_c) * 512 + tt * 128:
                                        (half * 16 + hd_c) * 512 + (tt + 1) * 128],
                                rhs=g[:, hd_c * 1024 + n2 * 512:hd_c * 1024 + (n2 + 1) * 512],
                                start=(half == 0 and hd_c == 0),
                                stop=(half == 1 and hd_c == 15))
                return r
            c.op("tensor", mmE, ins=[hm, g],
                 outs=[ps[i] for i in range(8)])
        for tt in range(4):
            for n2 in range(2):
                i = tt * 2 + n2
                c.op("vector", lambda q, i=i, n2=n2: q.tensor_tensor(
                    out=ybuf[:, n2 * 512:(n2 + 1) * 512], in0=ps[i][:, :],
                    in1=s3bc[:, n2 * 512:(n2 + 1) * 512], op=ALU.mult),
                    ins=[ps[i], s3bc], outs=[ybuf])
                c.op("vector", lambda q, tt=tt, n2=n2: q.tensor_tensor(
                    out=ybuf[:, n2 * 512:(n2 + 1) * 512],
                    in0=ybuf[:, n2 * 512:(n2 + 1) * 512],
                    in1=sx[:, tt * 1024 + n2 * 512:tt * 1024 + (n2 + 1) * 512],
                    op=ALU.add), ins=[ybuf, sx], outs=[ybuf])
            c.op("gpsimd", lambda q, tc=tc, tt=tt: q.dma_start(
                out=y_h[tc * 512 + tt * 128:tc * 512 + (tt + 1) * 128, :],
                in_=ybuf[:, :]), ins=[ybuf], dma=True)

    c.finish()
    return c.nc


def l1ffn_hostprep(w1_w, w1_s, w2_w, w2_s, h1_w, h1_s, h2_w, h2_s, w3_w, w3_s):
    f = np.ascontiguousarray
    return {
        "w1T": f(w1_w.T), "w2T": f(w2_w.T),
        "h1T": f(h1_w.T), "h2T": f(h2_w.T), "w3T": f(w3_w.T),
        "s1": f(w1_s.reshape(1, -1)), "s2": f(w2_s.reshape(1, -1)),
        "sh1": f(h1_s.reshape(1, -1)), "sh2": f(h2_s.reshape(1, -1)),
        "s3": f(w3_s.reshape(1, -1)),
    }


def run_launch(nc, shared, per_core, out_names):
    cst = const_inputs()
    in_maps = []
    for cid in range(NCORES):
        m = dict(shared)
        m.update(per_core[cid])
        m.update({k: v for k, v in cst.items() if k in ("c_ident", "c_ones", "c_iota")})
        in_maps.append(m)
    res = run_bass_kernel_spmd(nc, in_maps, core_ids=list(range(NCORES)))
    return [r for r in res.results]


_L1FFN_NC = None


def l1ffn_apply(x_full, weights):
    global _L1FFN_NC
    if _L1FFN_NC is None:
        _L1FFN_NC = build_l1ffn()
    shared = l1ffn_hostprep(*weights)
    xf = x_full.reshape(NTOK, D)
    per_core = [{"x": np.ascontiguousarray(xf[c * TPC:(c + 1) * TPC])}
                for c in range(NCORES)]
    rs = run_launch(_L1FFN_NC, shared, per_core, ["y"])
    y = np.concatenate([r["y"] for r in rs], axis=0)
    return y.reshape(B, S, D)



def build_l1attnb():
    c = Ctx("l1attnb")
    x_h = c.dram("x", [TPC, D])
    y_h = c.dram("y", [TPC, D], kind="ExternalOutput")
    BreT_h = c.dram("BreT", [1024, 1024])
    BimT_h = c.dram("BimT", [1024, 1024])
    CreT_h = c.dram("CreT", [1024, 1024])
    CimT_h = c.dram("CimT", [1024, 1024])
    plog_h = c.dram("plog", [3, 1024])
    lnw_h = c.dram("lnw", [1, 1024])
    lnb_h = c.dram("lnb", [1, 1024])
    dvec_h = c.dram("dvec", [1, 1024])
    decl_consts(c)
    PI = float(np.pi)

    ps = [c.ps("p%d" % i, [128, 512], F32) for i in range(6)]
    rowbuf = c.sb("rowbuf", [1, 4096], F32)
    stat = c.sb("stat", [128, 4], F32)
    sq = c.sb("sq", [128, 1024], F32)
    sx = c.sb("sx", [128, 1024], F32)
    xn = c.sb("xn", [128, 1024], F32)
    zf = c.sb("zf", [128, 1024], F32)
    Zt = c.sb("Zt", [128, 8 * 2048], BF16)
    cstage = c.sb("cstage", [128, 1024], F32)
    CbRe = c.sb("CbRe", [128, 8 * 1024], BF16)
    CbIm = c.sb("CbIm", [128, 8 * 1024], BF16)
    bstage = cstage
    bbre = c.sb("bbre", [128, 1024], BF16)
    bbim = c.sb("bbim", [128, 1024], BF16)
    ure = c.sb("ure", [128, 2048], F32)
    uim = c.sb("uim", [128, 2048], F32)
    HtRe = c.sb("HtRe", [128, 8 * 2048], BF16)
    HtIm = c.sb("HtIm", [128, 8 * 2048], BF16)
    scr = c.sb("scr", [128, 512], F32)
    scr2 = c.sb("scr2", [128, 512], F32)
    lnw_bc = c.sb("lnw_bc", [128, 1024], F32)
    lnb_bc = c.sb("lnb_bc", [128, 1024], F32)
    dv_bc = c.sb("dv_bc", [128, 1024], F32)
    ygel = c.sb("ygel", [128, 1024], F32)

    R = rowbuf
    for i in range(3):
        c.op("sync", lambda q, i=i: q.dma_start(out=R[0:1, i * 1024:(i + 1) * 1024],
             in_=plog_h[i:i + 1, :]), outs=[R], dma=True)
    c.op("scalar", lambda q: q.activation(out=R[0:1, 2048:3072], in_=R[0:1, 2048:3072],
         func=AF.Exp), ins=[R], outs=[R])
    gma_c = col_load_row(c, R, 2048, 1024, "gma_c", ps[0])
    c.op("scalar", lambda q: q.activation(out=R[0:1, 0:1024], in_=R[0:1, 0:1024],
         func=AF.Exp), ins=[R], outs=[R])
    c.op("scalar", lambda q: q.activation(out=R[0:1, 0:1024], in_=R[0:1, 0:1024],
         func=AF.Exp, scale=-1.0), ins=[R], outs=[R])
    c.op("scalar", lambda q: q.activation(out=R[0:1, 1024:2048], in_=R[0:1, 1024:2048],
         func=AF.Exp), ins=[R], outs=[R])
    c.op("vector", lambda q: q.tensor_scalar(out=R[0:1, 3072:4096], in0=R[0:1, 1024:2048],
         scalar1=PI / 2, scalar2=None, op0=ALU.add), ins=[R], outs=[R])
    c.op("vector", lambda q: q.tensor_scalar(out=R[0:1, 2048:3072], in0=R[0:1, 3072:4096],
         scalar1=PI, scalar2=2 * PI, op0=ALU.is_gt, op1=ALU.mult), ins=[R], outs=[R])
    c.op("vector", lambda q: q.tensor_tensor(out=R[0:1, 3072:4096], in0=R[0:1, 3072:4096],
         in1=R[0:1, 2048:3072], op=ALU.subtract), ins=[R], outs=[R])
    c.op("scalar", lambda q: q.activation(out=R[0:1, 3072:4096], in_=R[0:1, 3072:4096],
         func=AF.Sin), ins=[R], outs=[R])
    c.op("vector", lambda q: q.tensor_tensor(out=R[0:1, 3072:4096], in0=R[0:1, 3072:4096],
         in1=R[0:1, 0:1024], op=ALU.mult), ins=[R], outs=[R])
    lamre_c = col_load_row(c, R, 3072, 1024, "lamre_c", ps[0])
    c.op("vector", lambda q: q.tensor_scalar(out=R[0:1, 2048:3072], in0=R[0:1, 1024:2048],
         scalar1=PI, scalar2=2 * PI, op0=ALU.is_gt, op1=ALU.mult), ins=[R], outs=[R])
    c.op("vector", lambda q: q.tensor_tensor(out=R[0:1, 2048:3072], in0=R[0:1, 1024:2048],
         in1=R[0:1, 2048:3072], op=ALU.subtract), ins=[R], outs=[R])
    c.op("scalar", lambda q: q.activation(out=R[0:1, 2048:3072], in_=R[0:1, 2048:3072],
         func=AF.Sin), ins=[R], outs=[R])
    c.op("vector", lambda q: q.tensor_tensor(out=R[0:1, 2048:3072], in0=R[0:1, 2048:3072],
         in1=R[0:1, 0:1024], op=ALU.mult), ins=[R], outs=[R])
    lamim_c = col_load_row(c, R, 2048, 1024, "lamim_c", ps[0])
    nlamim_c = c.sb("nlamim_c", [128, 8], F32)
    c.op("vector", lambda q: q.tensor_scalar(out=nlamim_c[:, :], in0=lamim_c[:, :],
         scalar1=-1.0, scalar2=None, op0=ALU.mult), ins=[lamim_c], outs=[nlamim_c])

    for (h, bc) in ((lnw_h, lnw_bc), (lnb_h, lnb_bc), (dvec_h, dv_bc)):
        c.op("sync", lambda q, h=h: q.dma_start(out=R[0:1, 0:1024], in_=h[:, :]),
             outs=[R], dma=True)
        for i in range(2):
            row_bcast(c, R[0:1, i * 512:(i + 1) * 512], R,
                      bc[:, i * 512:(i + 1) * 512], bc, ps[0], 512)

    for (src, dst) in ((CreT_h, CbRe), (CimT_h, CbIm)):
        v = src.rearrange("(c p) m -> p c m", p=128)
        for j in range(8):
            c.op("sync", lambda q, v=v, j=j: q.dma_start(
                out=cstage[:, :], in_=v[:, j, :]), outs=[cstage], dma=True)
            c.op("scalar", lambda q, j=j, dst=dst: q.copy(
                out=dst[:, j * 1024:(j + 1) * 1024], in_=cstage[:, :]),
                ins=[cstage], outs=[dst])

    ZtV = Zt.rearrange("p (dc t) -> p dc t", t=2048)
    for tt in range(16):
        c.op("sync", lambda q, tt=tt: q.dma_start(out=sx[:, :],
             in_=x_h[tt * 128:(tt + 1) * 128, :]), outs=[sx], dma=True)
        layernorm_tile(c, sx[:, :], xn[:, :], sx, xn, stat, sq)
        layernorm_tile(c, xn[:, :], zf[:, :], xn, zf, stat, sq)
        c.op("vector", lambda q: q.tensor_tensor(out=zf[:, :], in0=zf[:, :],
             in1=lnw_bc[:, :], op=ALU.mult), ins=[zf, lnw_bc], outs=[zf])
        c.op("vector", lambda q: q.tensor_tensor(out=zf[:, :], in0=zf[:, :],
             in1=lnb_bc[:, :], op=ALU.add), ins=[zf, lnb_bc], outs=[zf])
        for dgrp in range(2):
            def mk_tr(q, dgrp=dgrp):
                for k in range(4):
                    r = q.transpose(out=ps[1][:, k * 128:(k + 1) * 128],
                                    in_=zf[:, (4 * dgrp + k) * 128:(4 * dgrp + k + 1) * 128],
                                    identity=c.s_ident[:, :])
                return r
            c.op("tensor", mk_tr, ins=[zf, c.s_ident], outs=[ps[1]])
            c.op("scalar", lambda q, dgrp=dgrp, tt=tt: q.copy(
                out=ZtV[:, 4 * dgrp:4 * dgrp + 4, tt * 128:(tt + 1) * 128],
                in_=ps[1][:, 0:512].rearrange("p (c t) -> p c t", t=128)),
                ins=[ps[1]], outs=[Zt])

    bV_re = BreT_h.rearrange("(c p) m -> p c m", p=128)
    bV_im = BimT_h.rearrange("(c p) m -> p c m", p=128)
    for ht in range(8):
        c.op("sync", lambda q, ht=ht: q.dma_start(
            out=bstage[:, 0:1024].rearrange("p (c m) -> p c m", m=128),
            in_=bV_re[:, 0:8, ht * 128:(ht + 1) * 128]), outs=[bstage], dma=True)
        c.op("scalar", lambda q: q.copy(out=bbre[:, :], in_=bstage[:, 0:1024]),
             ins=[bstage], outs=[bbre])
        c.op("sync", lambda q, ht=ht: q.dma_start(
            out=bstage[:, 0:1024].rearrange("p (c m) -> p c m", m=128),
            in_=bV_im[:, 0:8, ht * 128:(ht + 1) * 128]), outs=[bstage], dma=True)
        c.op("scalar", lambda q: q.copy(out=bbim[:, :], in_=bstage[:, 0:1024]),
             ins=[bstage], outs=[bbim])
        for tc in range(4):
            for (bb, ub, pp) in ((bbre, ure, ps[2]), (bbim, uim, ps[3])):
                def mmu(q, bb=bb, pp=pp, tc=tc):
                    for dc in range(8):
                        r = q.matmul(out=pp[:, :], lhsT=bb[:, dc * 128:(dc + 1) * 128],
                                     rhs=Zt[:, dc * 2048 + tc * 512:dc * 2048 + (tc + 1) * 512],
                                     start=(dc == 0), stop=(dc == 7))
                    return r
                c.op("tensor", mmu, ins=[bb, Zt], outs=[pp])
                c.op("vector", lambda q, pp=pp, ub=ub, tc=tc, ht=ht: q.tensor_scalar(
                    out=ub[:, tc * 512:(tc + 1) * 512], in0=pp[:, :],
                    scalar1=gma_c[:, ht:ht + 1], scalar2=None, op0=ALU.mult),
                    ins=[pp, gma_c], outs=[ub])
        for l in range(1, 4):
            lo = (l - 1) * 512
            cu = l * 512
            c.op("vector", lambda q, lo=lo, ht=ht: q.tensor_scalar(
                out=scr[:, :], in0=ure[:, lo:lo + 512],
                scalar1=lamre_c[:, ht:ht + 1], scalar2=None, op0=ALU.mult),
                ins=[ure, lamre_c], outs=[scr])
            c.op("vector", lambda q, lo=lo, ht=ht: q.scalar_tensor_tensor(
                out=scr2[:, :], in0=uim[:, lo:lo + 512],
                scalar=nlamim_c[:, ht:ht + 1], in1=scr[:, :],
                op0=ALU.mult, op1=ALU.add), ins=[uim, nlamim_c, scr], outs=[scr2])
            c.op("vector", lambda q, lo=lo, ht=ht: q.tensor_scalar(
                out=scr[:, :], in0=ure[:, lo:lo + 512],
                scalar1=lamim_c[:, ht:ht + 1], scalar2=None, op0=ALU.mult),
                ins=[ure, lamim_c], outs=[scr])
            c.op("vector", lambda q, cu=cu: q.tensor_tensor(
                out=ure[:, cu:cu + 512], in0=ure[:, cu:cu + 512], in1=scr2[:, :],
                op=ALU.add), ins=[scr2, ure], outs=[ure])
            c.op("vector", lambda q, lo=lo, ht=ht: q.scalar_tensor_tensor(
                out=scr2[:, :], in0=uim[:, lo:lo + 512],
                scalar=lamre_c[:, ht:ht + 1], in1=scr[:, :],
                op0=ALU.mult, op1=ALU.add), ins=[uim, lamre_c, scr], outs=[scr2])
            c.op("vector", lambda q, cu=cu: q.tensor_tensor(
                out=uim[:, cu:cu + 512], in0=uim[:, cu:cu + 512], in1=scr2[:, :],
                op=ALU.add), ins=[scr2, uim], outs=[uim])
        c.op("scalar", lambda q, ht=ht: q.copy(out=HtRe[:, ht * 2048:(ht + 1) * 2048],
             in_=ure[:, :]), ins=[ure], outs=[HtRe])
        c.op("vector", lambda q, ht=ht: q.tensor_scalar(
            out=HtIm[:, ht * 2048:(ht + 1) * 2048], in0=uim[:, :],
            scalar1=-1.0, scalar2=None, op0=ALU.mult), ins=[uim], outs=[HtIm])

    for tt in range(16):
        for mc in range(2):
            def mmy(q, tt=tt, mc=mc):
                for hc in range(8):
                    q.matmul(out=ps[4][:, :], lhsT=HtRe[:, hc * 2048 + tt * 128:hc * 2048 + (tt + 1) * 128],
                             rhs=CbRe[:, hc * 1024 + mc * 512:hc * 1024 + (mc + 1) * 512],
                             start=(hc == 0), stop=False)
                for hc in range(8):
                    r = q.matmul(out=ps[4][:, :], lhsT=HtIm[:, hc * 2048 + tt * 128:hc * 2048 + (tt + 1) * 128],
                                 rhs=CbIm[:, hc * 1024 + mc * 512:hc * 1024 + (mc + 1) * 512],
                                 start=False, stop=(hc == 7))
                return r
            c.op("tensor", mmy, ins=[HtRe, HtIm, CbRe, CbIm], outs=[ps[4]])
            c.op("scalar", lambda q, mc=mc: q.activation(
                out=ygel[:, mc * 512:(mc + 1) * 512], in_=ps[4][:, :],
                func=AF.Gelu), ins=[ps[4]], outs=[ygel])
        c.op("sync", lambda q, tt=tt: q.dma_start(out=sx[:, :],
             in_=x_h[tt * 128:(tt + 1) * 128, :]), outs=[sx], dma=True)
        layernorm_tile(c, sx[:, :], xn[:, :], sx, xn, stat, sq)
        c.op("vector", lambda q: q.tensor_tensor(out=xn[:, :], in0=xn[:, :],
             in1=dv_bc[:, :], op=ALU.mult), ins=[xn, dv_bc], outs=[xn])
        c.op("vector", lambda q: q.tensor_tensor(out=ygel[:, :], in0=ygel[:, :],
             in1=xn[:, :], op=ALU.add), ins=[ygel, xn], outs=[ygel])
        c.op("vector", lambda q: q.tensor_tensor(out=ygel[:, :], in0=ygel[:, :],
             in1=sx[:, :], op=ALU.add), ins=[ygel, sx], outs=[ygel])
        c.op("gpsimd", lambda q, tt=tt: q.dma_start(
            out=y_h[tt * 128:(tt + 1) * 128, :], in_=ygel[:, :]),
            ins=[ygel], dma=True)

    c.finish()
    return c.nc


def col_load_row(c, rowbuf, off, L, name, psum):
    nb = L // 128
    out = c.sb(name, [128, nb], F32)
    def mk(i):
        return lambda q: q.matmul(out=psum[:, i:i + 1],
                                  lhsT=rowbuf[0:1, off + i * 128:off + (i + 1) * 128],
                                  rhs=c.s_ones[0:1, 0:1], start=True, stop=True)
    for i in range(nb):
        c.op("tensor", mk(i), ins=[rowbuf, c.s_ones], outs=[psum])
    c.op("vector", lambda q: q.tensor_copy(out=out[:, :], in_=psum[:, 0:nb]),
         ins=[psum], outs=[out])
    return out


def l1attnb_hostprep(Bmat, Cmat, Dvec, params_log, ln_w, ln_b):
    f = np.ascontiguousarray
    return {
        "BreT": f(Bmat.real.T), "BimT": f(Bmat.imag.T),
        "CreT": f(Cmat.real.T), "CimT": f(Cmat.imag.T),
        "plog": f(params_log.reshape(3, 1024)),
        "lnw": f(ln_w.reshape(1, -1)), "lnb": f(ln_b.reshape(1, -1)),
        "dvec": f(Dvec.reshape(1, -1)),
    }


_L1ATTNB_NC = None


def l1attnb_apply(x_full, weights):
    global _L1ATTNB_NC
    if _L1ATTNB_NC is None:
        _L1ATTNB_NC = build_l1attnb()
    shared = l1attnb_hostprep(*weights)
    per_core = [{"x": np.ascontiguousarray(
        x_full[:, cid * 512:(cid + 1) * 512, :].reshape(TPC, D))}
        for cid in range(NCORES)]
    rs = run_launch(_L1ATTNB_NC, shared, per_core, ["y"])
    y = np.stack([r["y"].reshape(4, 512, D) for r in rs], axis=0)
    y = y.transpose(1, 0, 2, 3).reshape(B, S, D)
    return y



def build_l0attnb():
    c = Ctx("l0attnb")
    x_h = c.dram("x", [TPC, D])
    y_h = c.dram("y", [TPC, D], kind="ExternalOutput")
    Wq_h = c.dram("Wq", [1024, 1024])
    Wkv_h = c.dram("Wkv", [1024, 2048])
    Wo_h = c.dram("Wo", [1024, 1024])
    decl_consts(c)
    SCL = 1.0 / 8.0

    ps = [c.ps("p%d" % i, [128, 512], F32) for i in range(6)]
    stat = c.sb("stat", [128, 4], F32)
    sx = c.sb("sx", [128, 1024], F32)
    xn = c.sb("xn", [128, 1024], F32)
    Wob = c.sb("Wob", [128, 8 * 1024], BF16)
    qb = c.sb("qb", [128, 16 * 1024], BF16)
    kb = c.sb("kb", [128, 16 * 1024], BF16)
    vb = c.sb("vb", [128, 16 * 1024], BF16)
    prod = c.sb("prod", [128, 1024], F32)
    scb = c.sb("scb", [128, 4 * 16], F32)
    mxb = c.sb("mxb", [128, 16], F32)
    pex = c.sb("pex", [128, 1024], F32)
    vacc = c.sb("vacc", [128, 1024], F32)
    vT = c.sb("vT", [128, 8 * 128], BF16)
    wstage = c.sb("wstage", [128, 1024], F32)
    sq = prod
    yb = vacc
    Xt = c.sb("Xt", [128, 8 * 2048], BF16)

    XtV = Xt.rearrange("p (dc t) -> p dc t", t=2048)

    def load_weight(wh, dst, width):
        for dc in range(8):
            for j in range(width // 1024):
                c.op("sync", lambda q, wh=wh, dc=dc, j=j: q.dma_start(
                    out=wstage[:, :], in_=wh[dc * 128:(dc + 1) * 128, j * 1024:(j + 1) * 1024]),
                    outs=[wstage], dma=True)
                c.op("scalar", lambda q, dst=dst, dc=dc, j=j, width=width: q.copy(
                    out=dst[:, dc * width + j * 1024:dc * width + (j + 1) * 1024],
                    in_=wstage[:, :]), ins=[wstage], outs=[dst])

    load_weight(Wo_h, Wob, 1024)

    for tt in range(16):
        c.op("sync", lambda q, tt=tt: q.dma_start(out=sx[:, :],
             in_=x_h[tt * 128:(tt + 1) * 128, :]), outs=[sx], dma=True)
        layernorm_tile(c, sx[:, :], xn[:, :], sx, xn, stat, sq)
        for dgrp in range(2):
            def mk_tr(q, dgrp=dgrp):
                for k in range(4):
                    r = q.transpose(out=ps[0][:, k * 128:(k + 1) * 128],
                                    in_=xn[:, (4 * dgrp + k) * 128:(4 * dgrp + k + 1) * 128],
                                    identity=c.s_ident[:, :])
                return r
            c.op("tensor", mk_tr, ins=[xn, c.s_ident], outs=[ps[0]])
            c.op("scalar", lambda q, dgrp=dgrp, tt=tt: q.copy(
                out=XtV[:, 4 * dgrp:4 * dgrp + 4, tt * 128:(tt + 1) * 128],
                in_=ps[0][:, 0:512].rearrange("p (c t) -> p c t", t=128)),
                ins=[ps[0]], outs=[Xt])
    for (wh, wwidth, projs) in ((Wq_h, 1024, ((qb, 0),)),
                                (Wkv_h, 2048, ((kb, 0), (vb, 2)))):
        wsb = c.sb("wsb_%d" % wwidth, [128, 8 * wwidth], BF16)
        load_weight(wh, wsb, wwidth)
        for tt in range(16):
            for (dst, mc0) in projs:
                for mc in range(2):
                    pp = ps[1 + (mc % 2)]
                    def mm(q, tt=tt, wsb=wsb, wwidth=wwidth, mc=mc, mc0=mc0, pp=pp):
                        for dc in range(8):
                            r = q.matmul(out=pp[:, :],
                                         lhsT=Xt[:, dc * 2048 + tt * 128:dc * 2048 + (tt + 1) * 128],
                                         rhs=wsb[:, dc * wwidth + (mc0 + mc) * 512:dc * wwidth + (mc0 + mc + 1) * 512],
                                         start=(dc == 0), stop=(dc == 7))
                        return r
                    c.op("tensor", mm, ins=[Xt, wsb], outs=[pp])
                    c.op("scalar", lambda q, dst=dst, tt=tt, mc=mc, pp=pp: q.copy(
                        out=dst[:, tt * 1024 + mc * 512:tt * 1024 + (mc + 1) * 512],
                        in_=pp[:, :]), ins=[pp], outs=[dst])
        c.barrier()
        c.free(wsb)
    c.barrier()
    c.free(Xt)
    for i in range(4):
        for pt in range(4):
            qi = i * 4 + pt
            for j in range(4):
                kj = j * 4 + pt
                c.op("vector", lambda q, qi=qi, kj=kj: q.tensor_tensor(
                    out=prod[:, :], in0=qb[:, qi * 1024:(qi + 1) * 1024],
                    in1=kb[:, kj * 1024:(kj + 1) * 1024], op=ALU.mult),
                    ins=[qb, kb], outs=[prod])
                c.op("vector", lambda q, j=j: q.reduce_sum(
                    out=scb[:, j * 16:(j + 1) * 16],
                    in_=prod[:, :].rearrange("p (h d) -> p h d", d=64),
                    axis=AX.X), ins=[prod], outs=[scb])
            c.op("vector", lambda q: q.tensor_scalar(out=scb[:, :], in0=scb[:, :],
                 scalar1=SCL, scalar2=None, op0=ALU.mult), ins=[scb], outs=[scb])
            c.op("vector", lambda q: q.tensor_tensor(out=mxb[:, :], in0=scb[:, 0:16],
                 in1=scb[:, 16:32], op=ALU.max), ins=[scb], outs=[mxb])
            c.op("vector", lambda q: q.tensor_tensor(out=mxb[:, :], in0=mxb[:, :],
                 in1=scb[:, 32:48], op=ALU.max), ins=[scb, mxb], outs=[mxb])
            c.op("vector", lambda q: q.tensor_tensor(out=mxb[:, :], in0=mxb[:, :],
                 in1=scb[:, 48:64], op=ALU.max), ins=[scb, mxb], outs=[mxb])
            for j in range(4):
                c.op("vector", lambda q, j=j: q.tensor_tensor(
                    out=scb[:, j * 16:(j + 1) * 16], in0=scb[:, j * 16:(j + 1) * 16],
                    in1=mxb[:, :], op=ALU.subtract), ins=[scb, mxb], outs=[scb])
            c.op("scalar", lambda q: q.activation(out=scb[:, :], in_=scb[:, :],
                 func=AF.Exp), ins=[scb], outs=[scb])
            c.op("vector", lambda q: q.tensor_tensor(out=mxb[:, :], in0=scb[:, 0:16],
                 in1=scb[:, 16:32], op=ALU.add), ins=[scb], outs=[mxb])
            c.op("vector", lambda q: q.tensor_tensor(out=mxb[:, :], in0=mxb[:, :],
                 in1=scb[:, 32:48], op=ALU.add), ins=[scb, mxb], outs=[mxb])
            c.op("vector", lambda q: q.tensor_tensor(out=mxb[:, :], in0=mxb[:, :],
                 in1=scb[:, 48:64], op=ALU.add), ins=[scb, mxb], outs=[mxb])
            c.op("vector", lambda q: q.reciprocal(out=mxb[:, :], in_=mxb[:, :]),
                 ins=[mxb], outs=[mxb])
            for j in range(4):
                c.op("vector", lambda q, j=j: q.tensor_tensor(
                    out=scb[:, j * 16:(j + 1) * 16], in0=scb[:, j * 16:(j + 1) * 16],
                    in1=mxb[:, :], op=ALU.mult), ins=[scb, mxb], outs=[scb])
            for j in range(4):
                kj = j * 4 + pt
                c.op("scalar", lambda q, j=j: q.copy(
                    out=pex[:, :].rearrange("p (h d) -> p h d", d=64),
                    in_=scb[:, j * 16:(j + 1) * 16].rearrange(
                        "p (h o) -> p h o", o=1).broadcast_to([128, 16, 64])),
                    ins=[scb], outs=[pex])
                if j == 0:
                    c.op("vector", lambda q, kj=kj: q.tensor_tensor(
                        out=vacc[:, :], in0=vb[:, kj * 1024:(kj + 1) * 1024],
                        in1=pex[:, :], op=ALU.mult), ins=[vb, pex], outs=[vacc])
                else:
                    c.op("vector", lambda q, kj=kj: q.tensor_tensor(
                        out=prod[:, :], in0=vb[:, kj * 1024:(kj + 1) * 1024],
                        in1=pex[:, :], op=ALU.mult), ins=[vb, pex], outs=[prod])
                    c.op("vector", lambda q: q.tensor_tensor(
                        out=vacc[:, :], in0=vacc[:, :], in1=prod[:, :],
                        op=ALU.add), ins=[vacc, prod], outs=[vacc])
            for dgrp in range(2):
                def mk_tr2(q, dgrp=dgrp):
                    for k in range(4):
                        r = q.transpose(out=ps[0][:, k * 128:(k + 1) * 128],
                                        in_=vacc[:, (4 * dgrp + k) * 128:(4 * dgrp + k + 1) * 128],
                                        identity=c.s_ident[:, :])
                    return r
                c.op("tensor", mk_tr2, ins=[vacc, c.s_ident], outs=[ps[0]])
                c.op("scalar", lambda q, dgrp=dgrp: q.copy(
                    out=vT[:, :].rearrange("p (dc t) -> p dc t", t=128)[:, 4 * dgrp:4 * dgrp + 4, :],
                    in_=ps[0][:, 0:512].rearrange("p (c t) -> p c t", t=128)),
                    ins=[ps[0]], outs=[vT])
            c.op("sync", lambda q, qi=qi: q.dma_start(out=sx[:, :],
                 in_=x_h[qi * 128:(qi + 1) * 128, :]), outs=[sx], dma=True)
            for mc in range(2):
                pp = ps[2 + mc]
                def mmo(q, mc=mc, pp=pp):
                    for dc in range(8):
                        r = q.matmul(out=pp[:, :], lhsT=vT[:, dc * 128:(dc + 1) * 128],
                                     rhs=Wob[:, dc * 1024 + mc * 512:dc * 1024 + (mc + 1) * 512],
                                     start=(dc == 0), stop=(dc == 7))
                    return r
                c.op("tensor", mmo, ins=[vT, Wob], outs=[pp])
                c.op("vector", lambda q, mc=mc, pp=pp: q.tensor_tensor(
                    out=yb[:, mc * 512:(mc + 1) * 512], in0=pp[:, :],
                    in1=sx[:, mc * 512:(mc + 1) * 512], op=ALU.add),
                    ins=[pp, sx], outs=[yb])
            c.op("gpsimd", lambda q, qi=qi: q.dma_start(
                out=y_h[qi * 128:(qi + 1) * 128, :], in_=yb[:, :]),
                ins=[yb], dma=True)

    c.finish()
    return c.nc


_L0ATTNB_NC = None


def l0attnb_apply(x_full, weights):
    global _L0ATTNB_NC
    if _L0ATTNB_NC is None:
        _L0ATTNB_NC = build_l0attnb()
    Wq, Wkv, Wo = weights
    f = np.ascontiguousarray
    shared = {"Wq": f(Wq), "Wkv": f(Wkv), "Wo": f(Wo)}
    per_core = [{"x": np.ascontiguousarray(
        x_full[:, cid * 512:(cid + 1) * 512, :].reshape(TPC, D))}
        for cid in range(NCORES)]
    rs = run_launch(_L0ATTNB_NC, shared, per_core, ["y"])
    y = np.stack([r["y"].reshape(4, 512, D) for r in rs], axis=0)
    y = y.transpose(1, 0, 2, 3).reshape(B, S, D)
    return y



DBG_L1A = False


def build_l1attna():
    c = Ctx("l1attna")
    if DBG_L1A:
        dbg_h = c.dram("dbg", [128, 2048], kind="ExternalOutput")
    x_h = c.dram("x", [TPC, D])
    xkv_h = c.dram("xkv", [S, D])
    qoff_h = c.dram("qoff", [128, 1])
    y_h = c.dram("y", [TPC, D], kind="ExternalOutput")
    WQ_h = c.dram("WQ", [1024, 1024])
    WK_h = c.dram("WK", [1024, 1024])
    WV_h = c.dram("WV", [1024, 1024])
    decl_consts(c, iota=True, ones128=True)

    ps = [c.ps("p%d" % i, [128, 512], F32) for i in range(7)]
    smalls = c.sb("smalls", [128, 8], F32)
    stat = smalls
    sx = c.sb("sx", [128, 1024], F32)
    xn = c.sb("xn", [128, 1024], F32)
    prod = c.sb("prod", [128, 1024], F32)
    sq = prod
    qoff = smalls
    mvec = smalls
    rcol = smalls
    msk = c.sb("msk", [128, 256], F32)
    Pb = c.sb("Pb", [128, 256], BF16)
    wstage = c.sb("wstage", [128, 512], F32)
    qT = c.sb("qT", [128, 8 * 2048], BF16)
    kT = c.sb("kT", [128, 8 * 4096], BF16)
    Vb = c.sb("Vb", [128, 32 * 1024], BF16)
    yb = c.sb("yb", [128, 1024], F32)
    Xt = c.sb("Xt", [128, 8 * 128], BF16)
    wsbA = c.sb("wsbA", [128, 4 * 1024], BF16)
    wsbB = c.sb("wsbB", [128, 4 * 1024], BF16)

    c.op("sync", lambda q: q.dma_start(out=smalls[:, 4:5], in_=qoff_h[:, :]),
         outs=[smalls], dma=True)

    def load_w(wh, pair=0):
        for dc in range(8):
            wsb = wsbA if dc < 4 else wsbB
            dcl = dc % 4
            for j in range(2):
                c.op("sync", lambda q, wh=wh, dc=dc, j=j: q.dma_start(
                    out=wstage[:, :],
                    in_=wh[dc * 128:(dc + 1) * 128, j * 512:(j + 1) * 512]),
                    outs=[wstage], dma=True)
                c.op("scalar", lambda q, wsb=wsb, dcl=dcl, j=j: q.copy(
                    out=wsb[:, dcl * 1024 + j * 512:dcl * 1024 + (j + 1) * 512],
                    in_=wstage[:, :]), ins=[wstage], outs=[wsb])

    def ln_transpose(src_dram, src_tt):
        c.op("sync", lambda q: q.dma_start(out=sx[:, :],
             in_=src_dram[src_tt * 128:(src_tt + 1) * 128, :]), outs=[sx], dma=True)
        layernorm_tile(c, sx[:, :], xn[:, :], sx, xn, stat, sq)
        for dgrp in range(2):
            def mk_tr(q, dgrp=dgrp):
                for k in range(4):
                    r = q.transpose(out=ps[6][:, k * 128:(k + 1) * 128],
                                    in_=xn[:, (4 * dgrp + k) * 128:(4 * dgrp + k + 1) * 128],
                                    identity=c.s_ident[:, :])
                return r
            c.op("tensor", mk_tr, ins=[xn, c.s_ident], outs=[ps[6]])
            c.op("scalar", lambda q, dgrp=dgrp: q.copy(
                out=Xt[:, 4 * dgrp * 128:(4 * dgrp + 4) * 128],
                in_=ps[6][:, 0:512]), ins=[ps[6]], outs=[Xt])

    def proj_tile(pp, pair=0):
        for mc in range(2):
            def mm(q, mc=mc, pp=pp, pair=pair):
                for dc in range(8):
                    wsb = wsbA if dc < 4 else wsbB
                    dcl = dc % 4
                    r = q.matmul(out=pp[mc][:, :],
                                 lhsT=Xt[:, dc * 128:(dc + 1) * 128],
                                 rhs=wsb[:, dcl * 1024 + mc * 512:dcl * 1024 + (mc + 1) * 512],
                                 start=(dc == 0), stop=(dc == 7))
                return r
            c.op("tensor", mm, ins=[Xt, wsbA, wsbB], outs=[pp[mc]])

    def transpose_to(dst, dst_t0, src, src_h, cast_ps):
        for dgrp in range(2):
            def mk_tr(q, dgrp=dgrp):
                for k in range(4):
                    r = q.transpose(out=cast_ps[:, k * 128:(k + 1) * 128],
                                    in_=src[:, (4 * dgrp + k) * 128:(4 * dgrp + k + 1) * 128],
                                    identity=c.s_ident[:, :])
                return r
            c.op("tensor", mk_tr, ins=[src_h, c.s_ident], outs=[cast_ps])
            c.op("scalar", lambda q, dgrp=dgrp: q.copy(
                out=dst.rearrange("p (dc t) -> p dc t", t=dst.shape[1] // 8)[
                    :, 4 * dgrp:4 * dgrp + 4, dst_t0 * 128:(dst_t0 + 1) * 128],
                in_=cast_ps[:, 0:512].rearrange("p (c t) -> p c t", t=128)),
                ins=[cast_ps], outs=[dst])

    load_w(WQ_h)
    for tt in range(16):
        ln_transpose(x_h, tt)
        proj_tile((ps[0], ps[1]))
        c.op("vector", lambda q: q.tensor_copy(out=prod[:, 0:512], in_=ps[0][:, :]),
             ins=[ps[0]], outs=[prod])
        c.op("vector", lambda q: q.tensor_copy(out=prod[:, 512:1024], in_=ps[1][:, :]),
             ins=[ps[1]], outs=[prod])
        transpose_to(qT, tt, prod, prod, ps[6])

    load_w(WV_h)
    for gt in range(32):
        ln_transpose(xkv_h, gt)
        proj_tile((ps[0], ps[1]))
        c.op("scalar", lambda q, gt=gt: q.copy(
            out=Vb[:, gt * 1024:gt * 1024 + 512], in_=ps[0][:, :]),
            ins=[ps[0]], outs=[Vb])
        c.op("scalar", lambda q, gt=gt: q.copy(
            out=Vb[:, gt * 1024 + 512:(gt + 1) * 1024], in_=ps[1][:, :]),
            ins=[ps[1]], outs=[Vb])
    load_w(WK_h)
    for gt in range(32):
        ln_transpose(xkv_h, gt)
        proj_tile((ps[0], ps[1]))
        c.op("vector", lambda q: q.tensor_copy(out=prod[:, 0:512], in_=ps[0][:, :]),
             ins=[ps[0]], outs=[prod])
        c.op("vector", lambda q: q.tensor_copy(out=prod[:, 512:1024], in_=ps[1][:, :]),
             ins=[ps[1]], outs=[prod])
        transpose_to(kT, gt, prod, prod, ps[6])

    for qc in range(8):
        for kt in range(32):
            def mmS(q, qc=qc, kt=kt):
                for dc in range(8):
                    r = q.matmul(out=ps[0][:, 0:256],
                                 lhsT=kT[:, dc * 4096 + kt * 128:dc * 4096 + (kt + 1) * 128],
                                 rhs=qT[:, dc * 2048 + qc * 256:dc * 2048 + (qc + 1) * 256],
                                 start=(dc == 0), stop=(dc == 7))
                return r
            c.op("tensor", mmS, ins=[kT, qT], outs=[ps[0]])
            c.op("scalar", lambda q: q.activation(out=prod[:, 0:256],
                 in_=ps[0][:, 0:256], func=AF.Exp, scale=1.0 / 32.0),
                 ins=[ps[0]], outs=[prod])
            c.op("vector", lambda q, qc=qc, kt=kt: q.tensor_scalar(
                out=smalls[:, 5:6], in0=smalls[:, 4:5], scalar1=float(kt * 128 - qc * 256),
                scalar2=None, op0=ALU.add), ins=[smalls], outs=[smalls])
            c.op("vector", lambda q: q.tensor_scalar(out=msk[:, :],
                 in0=c.s_iota[:, 0:256], scalar1=smalls[:, 5:6], scalar2=None,
                 op0=ALU.is_ge), ins=[c.s_iota, smalls], outs=[msk])
            c.op("vector", lambda q: q.tensor_tensor(out=Pb[:, :],
                 in0=prod[:, 0:256], in1=msk[:, :], op=ALU.mult),
                 ins=[prod, msk], outs=[Pb])
            if DBG_L1A and qc == 0 and kt == 0:
                c.op("gpsimd", lambda q: q.dma_start(out=dbg_h[:, 0:256],
                     in_=prod[:, 0:256]), ins=[prod], dma=True)
                c.op("gpsimd", lambda q: q.dma_start(out=dbg_h[:, 256:512],
                     in_=msk[:, :]), ins=[msk], dma=True)
                c.op("vector", lambda q: q.tensor_copy(out=yb[:, 0:256],
                     in_=ps[0][:, 0:256]), ins=[ps[0]], outs=[yb])
                c.op("sync", lambda q: q.dma_start(out=dbg_h[:, 512:768],
                     in_=yb[:, 0:256]), ins=[yb], dma=True)
            def mmctx(q, kt=kt):
                for qs in range(2):
                    for vdc in range(2):
                        r = q.matmul(out=ps[2 + qs * 2 + vdc][:, :],
                                     lhsT=Pb[:, qs * 128:(qs + 1) * 128],
                                     rhs=Vb[:, kt * 1024 + vdc * 512:kt * 1024 + (vdc + 1) * 512],
                                     start=(kt == 0), stop=(kt == 31))
                return r
            c.op("tensor", mmctx, ins=[Pb, Vb],
                 outs=[ps[2], ps[3], ps[4], ps[5]])
            def mmden(q, kt=kt):
                for qs in range(2):
                    dp = ps[1] if qs == 0 else c.p_ones128
                    r = q.matmul(out=dp[:, 0:1],
                                 lhsT=Pb[:, qs * 128:(qs + 1) * 128],
                                 rhs=c.s_ones128[:, 0:1],
                                 start=(kt == 0), stop=(kt == 31))
                return r
            c.op("tensor", mmden, ins=[Pb, c.s_ones128],
                 outs=[ps[1], c.p_ones128])
        if DBG_L1A and qc == 0:
            c.op("vector", lambda q: q.tensor_copy(out=yb[:, 0:1],
                 in_=ps[1][:, 0:1]), ins=[ps[1]], outs=[yb])
            c.op("vector", lambda q: q.tensor_copy(out=yb[:, 1:2],
                 in_=c.p_ones128[:, 0:1]), ins=[c.p_ones128], outs=[yb])
            c.op("sync", lambda q: q.dma_start(out=dbg_h[:, 768:770],
                 in_=yb[:, 0:2]), ins=[yb], dma=True)
        c.op("vector", lambda q: q.reciprocal(out=smalls[:, 6:7], in_=ps[1][:, 0:1]),
             ins=[ps[1]], outs=[smalls])
        c.op("vector", lambda q: q.reciprocal(out=smalls[:, 7:8], in_=c.p_ones128[:, 0:1]),
             ins=[c.p_ones128], outs=[smalls])
        for qs in range(2):
            tg = qc * 256 + qs * 128
            c.op("sync", lambda q, tg=tg: q.dma_start(out=sx[:, :],
                 in_=x_h[tg:tg + 128, :]), outs=[sx], dma=True)
            for vdc in range(2):
                pp = ps[2 + qs * 2 + vdc]
                c.op("vector", lambda q, pp=pp, qs=qs, vdc=vdc: q.tensor_scalar(
                    out=yb[:, vdc * 512:(vdc + 1) * 512], in0=pp[:, :],
                    scalar1=smalls[:, 6 + qs:7 + qs], scalar2=None, op0=ALU.mult),
                    ins=[pp, smalls], outs=[yb])
                c.op("vector", lambda q, vdc=vdc: q.tensor_tensor(
                    out=yb[:, vdc * 512:(vdc + 1) * 512],
                    in0=yb[:, vdc * 512:(vdc + 1) * 512],
                    in1=sx[:, vdc * 512:(vdc + 1) * 512], op=ALU.add),
                    ins=[yb, sx], outs=[yb])
            c.op("gpsimd", lambda q, tg=tg: q.dma_start(
                out=y_h[tg:tg + 128, :], in_=yb[:, :]), ins=[yb], dma=True)

    c.finish()
    return c.nc


_L1ATTNA_NC = None


def l1attna_apply(x_full, weights):
    global _L1ATTNA_NC
    if _L1ATTNA_NC is None:
        _L1ATTNA_NC = build_l1attna()
    WQ, WK, WV = weights
    f = np.ascontiguousarray
    shared = {"WQ": f(WQ), "WK": f(WK), "WV": f(WV)}
    per_core = []
    for cid in range(NCORES):
        be, half = cid // 2, cid % 2
        per_core.append({
            "x": f(x_full[be, half * 2048:(half + 1) * 2048]),
            "xkv": f(x_full[be]),
            "qoff": (np.arange(128, dtype=np.float32)[:, None] - half * 2048),
        })
    rs = run_launch(_L1ATTNA_NC, shared, per_core, ["y"])
    y = np.empty((B, S, D), np.float32)
    for cid in range(NCORES):
        be, half = cid // 2, cid % 2
        y[be, half * 2048:(half + 1) * 2048] = rs[cid]["y"]
    if DBG_L1A:
        return y, [r["dbg"] for r in rs]
    return y



def build_l0attna():
    c = Ctx("l0attna")
    x_h = c.dram("x", [TPC, D])
    xkv_h = c.dram("xkv", [S, D])
    qoff_h = c.dram("qoff", [128, 1])
    y_h = c.dram("y", [TPC, D], kind="ExternalOutput")
    Wqk_h = c.dram("Wqk", [1024, 2048])
    Wv_h = c.dram("Wv", [1024, 1024])
    Wn_h = c.dram("Wn", [1024, 16])
    Wo_h = c.dram("Wo", [1024, 1024])
    bqk_h = c.dram("bqk", [1, 2048])
    bv_h = c.dram("bv", [1, 1024])
    bn_h = c.dram("bn", [1, 16])
    bo_h = c.dram("bo", [1, 1024])
    vscr_h = c.dram("vscr", [S, 2 * D], BF16, kind="Internal")
    kscr_h = c.dram("kscr", [1024, 2 * S], BF16, kind="Internal")
    decl_consts(c, iota=True)

    ps = [c.ps("p%d" % i, [128, 512], F32) for i in range(4)]
    psc = [c.ps("pc%d" % i, [128, 64], F32) for i in range(4)]
    smalls = c.sb("smalls", [128, 24], F32)
    stat = smalls
    sx = c.sb("sx", [128, 1024], F32)
    xn = c.sb("xn", [128, 1024], F32)
    prod = c.sb("prod", [128, 1024], F32)
    sq = prod
    ctxA = prod
    ctxB = xn
    mskc = c.sb("mskc", [128, 32 * 256], BF16)
    Pb = c.sb("Pb", [128, 256], BF16)
    nexp = smalls
    wstage = c.sb("wstage", [128, 512], F32)
    bb = c.sb("bb", [128, 2064], F32)
    qbd = c.sb("qbd", [128, 16 * 2048], BF16)
    qbdl = c.sb("qbdl", [128, 16 * 2048], BF16)
    qstg = c.sb("qstg", [128, 512], BF16)
    qstgl = c.sb("qstgl", [128, 512], BF16)
    Pf = c.sb("Pf", [128, 256], F32)
    Pbl = c.sb("Pbl", [128, 256], BF16)
    ksl = [c.sb("ksl%d" % i, [128, 256], BF16) for i in range(2)]
    yb = c.sb("yb", [128, 1024], F32)
    ctxT = c.sb("ctxT", [128, 8 * 128], BF16)
    ctxTl = c.sb("ctxTl", [128, 8 * 128], BF16)
    Xt = c.sb("Xt", [128, 8 * 128], BF16)
    wsbA = c.sb("wsbA", [128, 4 * 1024], BF16)
    wsbB = c.sb("wsbB", [128, 4 * 1024], BF16)
    wnb = c.sb("wnb", [128, 8 * 16], BF16)
    vstg = c.sb("vstg", [128, 1024], BF16)
    vstgl = c.sb("vstgl", [128, 1024], BF16)
    Vt2 = [c.sb("Vt%d" % i, [128, 128], BF16) for i in range(2)]

    c.op("sync", lambda q: q.dma_start(out=smalls[:, 4:5], in_=qoff_h[:, :]),
         outs=[smalls], dma=True)

    def bias_bcast(src_h, col0, dst_off, n):
        c.op("sync", lambda q: q.dma_start(out=wstage[0:1, 0:n],
             in_=src_h[0:1, col0:col0 + n]), outs=[wstage], dma=True)
        row_bcast(c, wstage[0:1, 0:n], wstage, bb[:, dst_off:dst_off + n], bb,
                  ps[0], n)

    def load_w(wh, col0):
        for dc in range(8):
            wsb = wsbA if dc < 4 else wsbB
            dcl = dc % 4
            for j in range(2):
                c.op("sync", lambda q, wh=wh, dc=dc, j=j, col0=col0: q.dma_start(
                    out=wstage[:, :],
                    in_=wh[dc * 128:(dc + 1) * 128, col0 + j * 512:col0 + (j + 1) * 512]),
                    outs=[wstage], dma=True)
                c.op("scalar", lambda q, wsb=wsb, dcl=dcl, j=j: q.copy(
                    out=wsb[:, dcl * 1024 + j * 512:dcl * 1024 + (j + 1) * 512],
                    in_=wstage[:, :]), ins=[wstage], outs=[wsb])

    def ln_transpose(src_dram, src_tt):
        c.op("sync", lambda q: q.dma_start(out=sx[:, :],
             in_=src_dram[src_tt * 128:(src_tt + 1) * 128, :]), outs=[sx], dma=True)
        layernorm_tile(c, sx[:, :], xn[:, :], sx, xn, stat, sq)
        for dgrp in range(2):
            def mk_tr(q, dgrp=dgrp):
                for k in range(4):
                    r = q.transpose(out=ps[3][:, k * 128:(k + 1) * 128],
                                    in_=xn[:, (4 * dgrp + k) * 128:(4 * dgrp + k + 1) * 128],
                                    identity=c.s_ident[:, :])
                return r
            c.op("tensor", mk_tr, ins=[xn, c.s_ident], outs=[ps[3]])
            c.op("scalar", lambda q, dgrp=dgrp: q.copy(
                out=Xt[:, 4 * dgrp * 128:(4 * dgrp + 4) * 128],
                in_=ps[3][:, 0:512]), ins=[ps[3]], outs=[Xt])

    def proj_tile(pp):
        for mc in range(2):
            def mm(q, mc=mc, pp=pp):
                for dc in range(8):
                    wsb = wsbA if dc < 4 else wsbB
                    dcl = dc % 4
                    r = q.matmul(out=pp[mc][:, :],
                                 lhsT=Xt[:, dc * 128:(dc + 1) * 128],
                                 rhs=wsb[:, dcl * 1024 + mc * 512:dcl * 1024 + (mc + 1) * 512],
                                 start=(dc == 0), stop=(dc == 7))
                return r
            c.op("tensor", mm, ins=[Xt, wsbA, wsbB], outs=[pp[mc]])

    def transpose_to(dst, dst_t0, src, src_h):
        n = dst.shape[1] // 8
        for dgrp in range(2):
            def mk_tr(q, dgrp=dgrp):
                for k in range(4):
                    r = q.transpose(out=ps[3][:, k * 128:(k + 1) * 128],
                                    in_=src[:, (4 * dgrp + k) * 128:(4 * dgrp + k + 1) * 128],
                                    identity=c.s_ident[:, :])
                return r
            c.op("tensor", mk_tr, ins=[src_h, c.s_ident], outs=[ps[3]])
            c.op("scalar", lambda q, dgrp=dgrp, dst_t0=dst_t0, n=n, dst=dst: q.copy(
                out=dst.rearrange("p (dc t) -> p dc t", t=n)[
                    :, 4 * dgrp:4 * dgrp + 4, dst_t0 * 128:(dst_t0 + 1) * 128],
                in_=ps[3][:, 0:512].rearrange("p (c t) -> p c t", t=128)),
                ins=[ps[3]], outs=[dst])

    for dc in range(8):
        c.op("sync", lambda q, dc=dc: q.dma_start(out=wstage[:, 0:16],
             in_=Wn_h[dc * 128:(dc + 1) * 128, :]), outs=[wstage], dma=True)
        c.op("scalar", lambda q, dc=dc: q.copy(out=wnb[:, dc * 16:(dc + 1) * 16],
             in_=wstage[:, 0:16]), ins=[wstage], outs=[wnb])
    bias_bcast(bn_h, 0, 2048, 16)
    bias_bcast(bqk_h, 0, 0, 512)
    bias_bcast(bqk_h, 512, 512, 512)
    c.op("vector", lambda q: q.memset(qbd[:, :], 0.0), outs=[qbd])
    c.op("vector", lambda q: q.memset(qbdl[:, :], 0.0), outs=[qbdl])
    load_w(Wqk_h, 0)
    for tt in range(16):
        ln_transpose(x_h, tt)
        proj_tile((ps[0], ps[1]))
        def mmn(q):
            for dc in range(8):
                r = q.matmul(out=ps[2][:, 0:16], lhsT=Xt[:, dc * 128:(dc + 1) * 128],
                             rhs=wnb[:, dc * 16:(dc + 1) * 16],
                             start=(dc == 0), stop=(dc == 7))
            return r
        c.op("tensor", mmn, ins=[Xt, wnb], outs=[ps[2]])
        c.op("vector", lambda q: q.tensor_tensor(out=nexp[:, 8:24], in0=ps[2][:, 0:16],
             in1=bb[:, 2048:2064], op=ALU.add), ins=[ps[2], bb], outs=[nexp])
        c.op("scalar", lambda q: q.activation(out=nexp[:, 8:24], in_=nexp[:, 8:24],
             func=AF.Exp, scale=-1.0), ins=[nexp], outs=[nexp])
        for mc in range(2):
            c.op("vector", lambda q, mc=mc: q.tensor_tensor(
                out=prod[:, mc * 512:(mc + 1) * 512], in0=ps[mc][:, :],
                in1=bb[:, mc * 512:(mc + 1) * 512], op=ALU.add),
                ins=[ps[mc], bb], outs=[prod])
        c.op("vector", lambda q: q.tensor_tensor(
            out=prod[:, :].rearrange("p (h d) -> p h d", d=64),
            in0=prod[:, :].rearrange("p (h d) -> p h d", d=64),
            in1=nexp[:, 8:24].rearrange("p (h o) -> p h o", o=1).broadcast_to([128, 16, 64]),
            op=ALU.mult), ins=[prod, nexp], outs=[prod])
        for dgrp in range(2):
            def mk_tr(q, dgrp=dgrp):
                for k in range(4):
                    r = q.transpose(out=ps[3][:, k * 128:(k + 1) * 128],
                                    in_=prod[:, (4 * dgrp + k) * 128:(4 * dgrp + k + 1) * 128],
                                    identity=c.s_ident[:, :])
                return r
            c.op("tensor", mk_tr, ins=[prod, c.s_ident], outs=[ps[3]])
            c.op("scalar", lambda q: q.copy(out=qstg[:, :], in_=ps[3][:, 0:512]),
                 ins=[ps[3]], outs=[qstg])
            c.op("vector", lambda q: q.tensor_tensor(out=qstgl[:, :],
                 in0=ps[3][:, 0:512], in1=qstg[:, :], op=ALU.subtract),
                 ins=[ps[3], qstg], outs=[qstgl])
            def evq(q, dgrp=dgrp, tt=tt):
                for k in range(4):
                    dc = 4 * dgrp + k
                    for (stg, dst) in ((qstg, qbd), (qstgl, qbdl)):
                        r = q.copy(out=dst[0:64, (2 * dc) * 2048 + tt * 128:
                                           (2 * dc) * 2048 + (tt + 1) * 128],
                                   in_=stg[0:64, k * 128:(k + 1) * 128])
                        r = q.copy(out=dst[64:128, (2 * dc + 1) * 2048 + tt * 128:
                                           (2 * dc + 1) * 2048 + (tt + 1) * 128],
                                   in_=stg[64:128, k * 128:(k + 1) * 128])
                return r
            c.op("scalar", evq, ins=[qstg, qstgl], outs=[qbd, qbdl])

    bias_bcast(bv_h, 0, 1024, 512)
    bias_bcast(bv_h, 512, 1536, 512)
    load_w(Wv_h, 0)
    for gt in range(32):
        ln_transpose(xkv_h, gt)
        proj_tile((ps[0], ps[1]))
        for mc in range(2):
            c.op("vector", lambda q, mc=mc: q.tensor_tensor(
                out=prod[:, mc * 512:(mc + 1) * 512], in0=ps[mc][:, :],
                in1=bb[:, 1024 + mc * 512:1024 + (mc + 1) * 512], op=ALU.add),
                ins=[ps[mc], bb], outs=[prod])
        c.op("scalar", lambda q: q.copy(out=vstg[:, :], in_=prod[:, :]),
             ins=[prod], outs=[vstg])
        c.op("vector", lambda q: q.tensor_tensor(out=vstgl[:, :], in0=prod[:, :],
             in1=vstg[:, :], op=ALU.subtract), ins=[prod, vstg], outs=[vstgl])
        c.op("gpsimd", lambda q, gt=gt: q.dma_start(
            out=vscr_h[gt * 128:(gt + 1) * 128, 0:1024], in_=vstg[:, :]),
            ins=[vstg], outs=[vscr_h], dma=True)
        c.op("gpsimd", lambda q, gt=gt: q.dma_start(
            out=vscr_h[gt * 128:(gt + 1) * 128, 1024:2048], in_=vstgl[:, :]),
            ins=[vstgl], outs=[vscr_h], dma=True)
    bias_bcast(bqk_h, 1024, 0, 512)
    bias_bcast(bqk_h, 1536, 512, 512)
    load_w(Wqk_h, 1024)
    for gt in range(32):
        ln_transpose(xkv_h, gt)
        proj_tile((ps[0], ps[1]))
        for mc in range(2):
            c.op("vector", lambda q, mc=mc: q.tensor_tensor(
                out=prod[:, mc * 512:(mc + 1) * 512], in0=ps[mc][:, :],
                in1=bb[:, mc * 512:(mc + 1) * 512], op=ALU.add),
                ins=[ps[mc], bb], outs=[prod])
        for dgrp in range(2):
            def mk_tr(q, dgrp=dgrp):
                for k in range(4):
                    r = q.transpose(out=ps[3][:, k * 128:(k + 1) * 128],
                                    in_=prod[:, (4 * dgrp + k) * 128:(4 * dgrp + k + 1) * 128],
                                    identity=c.s_ident[:, :])
                return r
            c.op("tensor", mk_tr, ins=[prod, c.s_ident], outs=[ps[3]])
            c.op("scalar", lambda q: q.copy(out=qstg[:, :], in_=ps[3][:, 0:512]),
                 ins=[ps[3]], outs=[qstg])
            c.op("vector", lambda q: q.tensor_tensor(out=qstgl[:, :],
                 in0=ps[3][:, 0:512], in1=qstg[:, :], op=ALU.subtract),
                 ins=[ps[3], qstg], outs=[qstgl])
            for k in range(4):
                dc = 4 * dgrp + k
                c.op("gpsimd", lambda q, dc=dc, k=k, gt=gt: q.dma_start(
                    out=kscr_h[dc * 128:(dc + 1) * 128, gt * 128:(gt + 1) * 128],
                    in_=qstg[:, k * 128:(k + 1) * 128]),
                    ins=[qstg], outs=[kscr_h], dma=True)
                c.op("gpsimd", lambda q, dc=dc, k=k, gt=gt: q.dma_start(
                    out=kscr_h[dc * 128:(dc + 1) * 128, S + gt * 128:S + (gt + 1) * 128],
                    in_=qstgl[:, k * 128:(k + 1) * 128]),
                    ins=[qstgl], outs=[kscr_h], dma=True)

    bias_bcast(bo_h, 0, 0, 512)
    bias_bcast(bo_h, 512, 512, 512)
    load_w(Wo_h, 0)
    for qc in range(8):
        for kt in range(32):
            c.op("vector", lambda q, qc=qc, kt=kt: q.tensor_scalar(
                out=smalls[:, 5:6], in0=smalls[:, 4:5],
                scalar1=float(kt * 128 - qc * 256), scalar2=None, op0=ALU.add),
                ins=[smalls], outs=[smalls])
            c.op("vector", lambda q, kt=kt: q.tensor_scalar(
                out=mskc[:, kt * 256:(kt + 1) * 256], in0=c.s_iota[:, 0:256],
                scalar1=smalls[:, 5:6], scalar2=None, op0=ALU.is_ge),
                ins=[c.s_iota, smalls], outs=[mskc])
        for h in range(16):
            for kt in range(32):
                vt = Vt2[kt % 2]
                c.op("sync", lambda q, vt=vt, kt=kt, h=h: q.dma_start(
                    out=vt[:, 0:64],
                    in_=vscr_h[kt * 128:(kt + 1) * 128, h * 64:(h + 1) * 64]),
                    ins=[vscr_h], outs=[vt], dma=True)
                c.op("sync", lambda q, vt=vt, kt=kt, h=h: q.dma_start(
                    out=vt[:, 64:128],
                    in_=vscr_h[kt * 128:(kt + 1) * 128, 1024 + h * 64:1024 + (h + 1) * 64]),
                    ins=[vscr_h], outs=[vt], dma=True)
                kb2 = ksl[kt % 2]
                c.op("sync", lambda q, kb2=kb2, h=h, kt=kt: q.dma_start(
                    out=kb2[:, 0:128],
                    in_=kscr_h[(h // 2) * 128:(h // 2 + 1) * 128,
                               kt * 128:(kt + 1) * 128]),
                    ins=[kscr_h], outs=[kb2], dma=True)
                c.op("sync", lambda q, kb2=kb2, h=h, kt=kt: q.dma_start(
                    out=kb2[:, 128:256],
                    in_=kscr_h[(h // 2) * 128:(h // 2 + 1) * 128,
                               S + kt * 128:S + (kt + 1) * 128]),
                    ins=[kscr_h], outs=[kb2], dma=True)
                sps = ps[kt % 2]
                if kt % 2 == 0:
                    pf, pfh = Pf[:, :], Pf
                    pb, pbh = Pb[:, :], Pb
                    pbl, pblh = Pbl[:, :], Pbl
                else:
                    pf, pfh = wstage[:, 0:256], wstage
                    pb, pbh = qstg[:, 0:256], qstg
                    pbl, pblh = qstgl[:, 0:256], qstgl
                def mmS(q, qc=qc, h=h, kt=kt, kb2=kb2, sps=sps):
                    q.matmul(out=sps[:, 0:256], lhsT=kb2[:, 0:128],
                             rhs=qbd[:, h * 2048 + qc * 256:h * 2048 + (qc + 1) * 256],
                             start=True, stop=False)
                    q.matmul(out=sps[:, 0:256], lhsT=kb2[:, 0:128],
                             rhs=qbdl[:, h * 2048 + qc * 256:h * 2048 + (qc + 1) * 256],
                             start=False, stop=False)
                    return q.matmul(out=sps[:, 0:256], lhsT=kb2[:, 128:256],
                             rhs=qbd[:, h * 2048 + qc * 256:h * 2048 + (qc + 1) * 256],
                             start=False, stop=True)
                c.op("tensor", mmS, ins=[kb2, qbd, qbdl], outs=[sps])
                c.op("vector", lambda q, kt=kt, sps=sps, pf=pf: q.tensor_tensor(
                    out=pf, in0=sps[:, 0:256],
                    in1=mskc[:, kt * 256:(kt + 1) * 256],
                    op=ALU.mult), ins=[sps, mskc], outs=[pfh])
                c.op("scalar", lambda q, pf=pf, pb=pb: q.copy(
                     out=pb, in_=pf), ins=[pfh], outs=[pbh])
                c.op("vector", lambda q, pf=pf, pb=pb, pbl=pbl: q.tensor_tensor(
                     out=pbl, in0=pf, in1=pb,
                     op=ALU.subtract), ins=[pfh, pbh], outs=[pblh])
                def mmctx(q, kt=kt, vt=vt, pb=pb, pbl=pbl):
                    for qs in range(2):
                        q.matmul(out=psc[qs][:, :],
                                 lhsT=pb[:, qs * 128:(qs + 1) * 128],
                                 rhs=vt[:, 0:64],
                                 start=(kt == 0), stop=False)
                        q.matmul(out=psc[qs][:, :],
                                 lhsT=pbl[:, qs * 128:(qs + 1) * 128],
                                 rhs=vt[:, 0:64],
                                 start=False, stop=False)
                        r = q.matmul(out=psc[qs][:, :],
                                 lhsT=pb[:, qs * 128:(qs + 1) * 128],
                                 rhs=vt[:, 64:128],
                                 start=False, stop=(kt == 31))
                    return r
                c.op("tensor", mmctx, ins=[pbh, pblh, vt], outs=[psc[0], psc[1]])
            for qs in range(2):
                dstb = ctxA if qs == 0 else ctxB
                c.op("vector", lambda q, dstb=dstb, h=h, qs=qs: q.tensor_copy(
                    out=dstb[:, h * 64:(h + 1) * 64],
                    in_=psc[qs][:, :]),
                    ins=[psc[qs]], outs=[dstb])
        for qs in range(2):
            srcb = ctxA if qs == 0 else ctxB
            for dgrp in range(2):
                def mk_tr3(q, dgrp=dgrp, srcb=srcb):
                    for k in range(4):
                        r = q.transpose(out=ps[3][:, k * 128:(k + 1) * 128],
                                        in_=srcb[:, (4 * dgrp + k) * 128:(4 * dgrp + k + 1) * 128],
                                        identity=c.s_ident[:, :])
                    return r
                c.op("tensor", mk_tr3, ins=[srcb, c.s_ident], outs=[ps[3]])
                c.op("scalar", lambda q, dgrp=dgrp: q.copy(
                    out=ctxT[:, dgrp * 512:(dgrp + 1) * 512], in_=ps[3][:, 0:512]),
                    ins=[ps[3]], outs=[ctxT])
                c.op("vector", lambda q, dgrp=dgrp: q.tensor_tensor(
                    out=ctxTl[:, dgrp * 512:(dgrp + 1) * 512], in0=ps[3][:, 0:512],
                    in1=ctxT[:, dgrp * 512:(dgrp + 1) * 512], op=ALU.subtract),
                    ins=[ps[3], ctxT], outs=[ctxTl])
            tg = qc * 256 + qs * 128
            c.op("sync", lambda q, tg=tg: q.dma_start(out=sx[:, :],
                 in_=x_h[tg:tg + 128, :]), outs=[sx], dma=True)
            for mc in range(2):
                pp = ps[mc]
                def mmo(q, mc=mc, pp=pp):
                    for dc in range(8):
                        wsb = wsbA if dc < 4 else wsbB
                        dcl = dc % 4
                        q.matmul(out=pp[:, :], lhsT=ctxT[:, dc * 128:(dc + 1) * 128],
                                 rhs=wsb[:, dcl * 1024 + mc * 512:dcl * 1024 + (mc + 1) * 512],
                                 start=(dc == 0), stop=False)
                        r = q.matmul(out=pp[:, :], lhsT=ctxTl[:, dc * 128:(dc + 1) * 128],
                                 rhs=wsb[:, dcl * 1024 + mc * 512:dcl * 1024 + (mc + 1) * 512],
                                 start=False, stop=(dc == 7))
                    return r
                c.op("tensor", mmo, ins=[ctxT, ctxTl, wsbA, wsbB], outs=[pp])
                c.op("vector", lambda q, mc=mc, pp=pp: q.tensor_tensor(
                    out=yb[:, mc * 512:(mc + 1) * 512], in0=pp[:, :],
                    in1=bb[:, mc * 512:(mc + 1) * 512], op=ALU.add),
                    ins=[pp, bb], outs=[yb])
                c.op("vector", lambda q, mc=mc: q.tensor_tensor(
                    out=yb[:, mc * 512:(mc + 1) * 512],
                    in0=yb[:, mc * 512:(mc + 1) * 512],
                    in1=sx[:, mc * 512:(mc + 1) * 512], op=ALU.add),
                    ins=[yb, sx], outs=[yb])
            c.op("gpsimd", lambda q, tg=tg: q.dma_start(
                out=y_h[tg:tg + 128, :], in_=yb[:, :]), ins=[yb], dma=True)
    c.finish()
    return c.nc


_L0ATTNA_NC = None


def l0attna_apply(x_full, weights):
    global _L0ATTNA_NC
    if _L0ATTNA_NC is None:
        _L0ATTNA_NC = build_l0attna()
    Wqk, bqk, Wv, bv, Wn, bn, Wo, bo = weights
    f = np.ascontiguousarray
    shared = {"Wqk": f(Wqk), "Wv": f(Wv), "Wn": f(Wn), "Wo": f(Wo),
              "bqk": f(bqk.reshape(1, -1)), "bv": f(bv.reshape(1, -1)),
              "bn": f(bn.reshape(1, -1)), "bo": f(bo.reshape(1, -1))}
    per_core = []
    for cid in range(NCORES):
        be, half = cid // 2, cid % 2
        per_core.append({
            "x": f(x_full[be, half * 2048:(half + 1) * 2048]),
            "xkv": f(x_full[be]),
            "qoff": (np.arange(128, dtype=np.float32)[:, None] - half * 2048),
        })
    rs = run_launch(_L0ATTNA_NC, shared, per_core, ["y"])
    y = np.empty((B, S, D), np.float32)
    for cid in range(NCORES):
        be, half = cid // 2, cid % 2
        y[be, half * 2048:(half + 1) * 2048] = rs[cid]["y"]
    return y



def build_l0ffn():
    c = Ctx("l0ffn")
    x_h = c.dram("x", [TPC, D])
    xkv_h = c.dram("xkv", [S, D])
    qoff_h = c.dram("qoff", [128, 1])
    y_h = c.dram("y", [TPC, D], kind="ExternalOutput")
    WqT_h = c.dram("WqT", [1024, 1024])
    WdkvT_h = c.dram("WdkvT", [1024, 512])
    WukT_h = c.dram("WukT", [512, 1024])
    WuvT_h = c.dram("WuvT", [512, 1024])
    WoT_h = c.dram("WoT", [1024, 1024])
    n1_h = c.dram("n1", [1, 1024])
    n2_h = c.dram("n2", [1, 1024])
    lng_h = c.dram("lng", [1, 512])
    lnb_h = c.dram("lnb", [1, 512])
    gwT_h = c.dram("gwT", [1024, 4])
    gb_h = c.dram("gb", [1, 4])
    W1T_h = c.dram("W1T", [4, 1024, 4096])
    b1_h = c.dram("b1", [4, 4096])
    W2T_h = c.dram("W2T", [4, 4096, 1024])
    b2_h = c.dram("b2", [4, 1024])
    cosq_h = c.dram("cosq", [TPC, 512])
    sinq_h = c.dram("sinq", [TPC, 512])
    coskv_h = c.dram("coskv", [S, 512])
    sinkv_h = c.dram("sinkv", [S, 512])
    vscr_h = c.dram("vscr", [S, D], BF16, kind="Internal")
    x1scr_h = c.dram("x1scr", [TPC, D], kind="Internal")
    decl_consts(c, iota=True)

    ps = [c.ps("p%d" % i, [128, 512], F32) for i in range(4)]
    smalls = c.sb("smalls", [128, 8], F32)
    stat = smalls
    sx = c.sb("sx", [128, 1024], F32)
    xn = c.sb("xn", [128, 1024], F32)
    prod = c.sb("prod", [128, 1024], F32)
    sq = prod
    rmsf = c.sb("rmsf", [128, 1024], F32)
    csb = c.sb("csb", [128, 1024], F32)
    r1 = c.sb("r1", [128, 512], F32)
    r2 = c.sb("r2", [128, 512], F32)
    Pf = c.sb("Pf", [128, 256], F32)
    Pb = c.sb("Pb", [128, 256], BF16)
    PfB = c.sb("PfB", [128, 256], F32)
    PbB = c.sb("PbB", [128, 256], BF16)
    qstg = c.sb("qstg", [128, 512], BF16)
    qslE = c.sb("qslE", [128, 256], BF16)
    qslO = c.sb("qslO", [128, 256], BF16)
    wstage = c.sb("wstage", [128, 512], F32)
    bb = c.sb("bb", [128, 3 * 1024 + 64], F32)
    yb = c.sb("yb", [128, 1024], F32)
    ctxA = c.sb("ctxA", [128, 1024], F32)
    ctxB = c.sb("ctxB", [128, 1024], F32)
    ctxT = c.sb("ctxT", [128, 8 * 128], BF16)
    Xt = c.sb("Xt", [128, 8 * 128], BF16)
    latf = c.sb("latf", [128, 512], F32)
    latn = c.sb("latn", [128, 512], F32)
    sq2 = c.sb("sq2", [128, 512], F32)
    latT = c.sb("latT", [128, 4 * 128], BF16)
    vstg = c.sb("vstg", [128, 1024], BF16)
    Vt2 = [c.sb("Vt%d" % i, [128, 64], BF16) for i in range(2)]
    s_ones128 = c.sb("s_ones128b", [128, 128], BF16)
    mskc = c.sb("mskc", [128, 32 * 256], BF16)
    wdkvb = c.sb("wdkvb", [128, 8 * 512], BF16)
    wukb = c.sb("wukb", [128, 4 * 1024], BF16)
    wuvb = c.sb("wuvb", [128, 4 * 1024], BF16)
    wsbA = c.sb("wsbA", [128, 4 * 1024], BF16)
    wsbB = c.sb("wsbB", [128, 4 * 1024], BF16)
    kT = c.sb("kT", [128, 8 * 4096], BF16)
    qscr_h = c.dram("qscr", [1024, TPC], BF16, kind="Internal")

    c.op("sync", lambda q: q.dma_start(out=smalls[:, 4:5], in_=qoff_h[:, :]),
         outs=[smalls], dma=True)
    c.op("tensor", lambda q: q.matmul(out=ps[0][:, 0:128], lhsT=c.s_ones[0:1, 0:128],
         rhs=c.s_ones[0:1, 0:128], start=True, stop=True), ins=[c.s_ones], outs=[ps[0]])
    c.op("vector", lambda q: q.tensor_copy(out=s_ones128[:, :], in_=ps[0][:, 0:128]),
         ins=[ps[0]], outs=[s_ones128])

    def bias_bcast(src_h, col0, dst_off, n):
        c.op("sync", lambda q: q.dma_start(out=wstage[0:1, 0:n],
             in_=src_h[0:1, col0:col0 + n]), outs=[wstage], dma=True)
        row_bcast(c, wstage[0:1, 0:n], wstage, bb[:, dst_off:dst_off + n], bb,
                  ps[0], n)

    bias_bcast(n1_h, 0, 0, 512)
    bias_bcast(n1_h, 512, 512, 512)
    bias_bcast(n2_h, 0, 1024, 512)
    bias_bcast(n2_h, 512, 1536, 512)
    bias_bcast(lng_h, 0, 2048, 512)
    bias_bcast(lnb_h, 0, 2560, 512)
    bias_bcast(gb_h, 0, 3072, 4)

    def load_w(wh, col0):
        for dc in range(8):
            wsb = wsbA if dc < 4 else wsbB
            dcl = dc % 4
            for j in range(2):
                c.op("sync", lambda q, wh=wh, dc=dc, j=j, col0=col0: q.dma_start(
                    out=wstage[:, :],
                    in_=wh[dc * 128:(dc + 1) * 128, col0 + j * 512:col0 + (j + 1) * 512]),
                    outs=[wstage], dma=True)
                c.op("scalar", lambda q, wsb=wsb, dcl=dcl, j=j: q.copy(
                    out=wsb[:, dcl * 1024 + j * 512:dcl * 1024 + (j + 1) * 512],
                    in_=wstage[:, :]), ins=[wstage], outs=[wsb])

    def load_small(wh, dst, rows, cols):
        for dc in range(rows // 128):
            for j in range(0, cols, 512):
                n = min(512, cols - j)
                c.op("sync", lambda q, wh=wh, dc=dc, j=j, n=n: q.dma_start(
                    out=wstage[:, 0:n],
                    in_=wh[dc * 128:(dc + 1) * 128, j:j + n]), outs=[wstage], dma=True)
                c.op("scalar", lambda q, dst=dst, dc=dc, j=j, n=n, cols=cols: q.copy(
                    out=dst[:, dc * cols + j:dc * cols + j + n],
                    in_=wstage[:, 0:n]), ins=[wstage], outs=[dst])

    def ln_rms_transpose(src_dram, src_tt, nbc_off):
        c.op("sync", lambda q: q.dma_start(out=sx[:, :],
             in_=src_dram[src_tt * 128:(src_tt + 1) * 128, :]), outs=[sx], dma=True)
        layernorm_tile(c, sx[:, :], xn[:, :], sx, xn, stat, sq)
        layernorm_tile(c, xn[:, :], rmsf[:, :], xn, rmsf, stat, sq, eps=1e-6,
                       rms=True)
        c.op("vector", lambda q: q.tensor_tensor(out=rmsf[:, :], in0=rmsf[:, :],
             in1=bb[:, nbc_off:nbc_off + 1024], op=ALU.mult),
             ins=[rmsf, bb], outs=[rmsf])
        for dgrp in range(2):
            def mk_tr(q, dgrp=dgrp):
                for k in range(4):
                    r = q.transpose(out=ps[3][:, k * 128:(k + 1) * 128],
                                    in_=rmsf[:, (4 * dgrp + k) * 128:(4 * dgrp + k + 1) * 128],
                                    identity=c.s_ident[:, :])
                return r
            c.op("tensor", mk_tr, ins=[rmsf, c.s_ident], outs=[ps[3]])
            c.op("scalar", lambda q, dgrp=dgrp: q.copy(
                out=Xt[:, 4 * dgrp * 128:(4 * dgrp + 4) * 128],
                in_=ps[3][:, 0:512]), ins=[ps[3]], outs=[Xt])

    def proj_tile(pp):
        for mc in range(2):
            def mm(q, mc=mc, pp=pp):
                for dc in range(8):
                    wsb = wsbA if dc < 4 else wsbB
                    dcl = dc % 4
                    r = q.matmul(out=pp[mc][:, :],
                                 lhsT=Xt[:, dc * 128:(dc + 1) * 128],
                                 rhs=wsb[:, dcl * 1024 + mc * 512:dcl * 1024 + (mc + 1) * 512],
                                 start=(dc == 0), stop=(dc == 7))
                return r
            c.op("tensor", mm, ins=[Xt, wsbA, wsbB], outs=[pp[mc]])

    def rope(cos_h, sin_h, t0, src, src_h, dst, dst_h):
        c.op("sync", lambda q: q.dma_start(out=csb[:, 0:512],
             in_=cos_h[t0:t0 + 128, :]), outs=[csb], dma=True)
        c.op("sync", lambda q: q.dma_start(out=csb[:, 512:1024],
             in_=sin_h[t0:t0 + 128, :]), outs=[csb], dma=True)
        x1v = src.rearrange("p (h d) -> p h d", d=64)[:, :, 0:32]
        x2v = src.rearrange("p (h d) -> p h d", d=64)[:, :, 32:64]
        cv = csb[:, 0:512].rearrange("p (h f) -> p h f", f=32)
        sv = csb[:, 512:1024].rearrange("p (h f) -> p h f", f=32)
        r1v = r1[:, :].rearrange("p (h f) -> p h f", f=32)
        r2v = r2[:, :].rearrange("p (h f) -> p h f", f=32)
        d1v = dst.rearrange("p (h d) -> p h d", d=64)[:, :, 0:32]
        d2v = dst.rearrange("p (h d) -> p h d", d=64)[:, :, 32:64]
        c.op("vector", lambda q: q.tensor_tensor(out=r1v, in0=x1v, in1=cv,
             op=ALU.mult), ins=[src_h, csb], outs=[r1])
        c.op("vector", lambda q: q.tensor_tensor(out=r2v, in0=x2v, in1=sv,
             op=ALU.mult), ins=[src_h, csb], outs=[r2])
        c.op("vector", lambda q: q.tensor_tensor(out=d1v, in0=r1v, in1=r2v,
             op=ALU.subtract), ins=[r1, r2], outs=[dst_h])
        c.op("vector", lambda q: q.tensor_tensor(out=r1v, in0=x1v, in1=sv,
             op=ALU.mult), ins=[src_h, csb], outs=[r1])
        c.op("vector", lambda q: q.tensor_tensor(out=r2v, in0=x2v, in1=cv,
             op=ALU.mult), ins=[src_h, csb], outs=[r2])
        c.op("vector", lambda q: q.tensor_tensor(out=d2v, in0=r1v, in1=r2v,
             op=ALU.add), ins=[r1, r2], outs=[dst_h])

    c.op("vector", lambda q: q.memset(qslE[:, :], 0.0), outs=[qslE])
    c.op("vector", lambda q: q.memset(qslO[:, :], 0.0), outs=[qslO])
    load_w(WqT_h, 0)
    for tt in range(16):
        ln_rms_transpose(x_h, tt, 0)
        proj_tile((ps[0], ps[1]))
        c.op("vector", lambda q: q.tensor_copy(out=prod[:, 0:512], in_=ps[0][:, :]),
             ins=[ps[0]], outs=[prod])
        c.op("vector", lambda q: q.tensor_copy(out=prod[:, 512:1024], in_=ps[1][:, :]),
             ins=[ps[1]], outs=[prod])
        rope(cosq_h, sinq_h, tt * 128, prod[:, :], prod, rmsf[:, :], rmsf)
        for dgrp in range(2):
            def mk_tr2(q, dgrp=dgrp):
                for k in range(4):
                    r = q.transpose(out=ps[3][:, k * 128:(k + 1) * 128],
                                    in_=rmsf[:, (4 * dgrp + k) * 128:(4 * dgrp + k + 1) * 128],
                                    identity=c.s_ident[:, :])
                return r
            c.op("tensor", mk_tr2, ins=[rmsf, c.s_ident], outs=[ps[3]])
            c.op("scalar", lambda q: q.copy(out=qstg[:, :], in_=ps[3][:, 0:512]),
                 ins=[ps[3]], outs=[qstg])
            for k in range(4):
                dc = 4 * dgrp + k
                c.op("gpsimd", lambda q, dc=dc, k=k, tt=tt: q.dma_start(
                    out=qscr_h[dc * 128:(dc + 1) * 128, tt * 128:(tt + 1) * 128],
                    in_=qstg[:, k * 128:(k + 1) * 128]),
                    ins=[qstg], outs=[qscr_h], dma=True)

    load_small(WdkvT_h, wdkvb, 1024, 512)
    load_small(WukT_h, wukb, 512, 1024)
    load_small(WuvT_h, wuvb, 512, 1024)
    for gt in range(32):
        ln_rms_transpose(xkv_h, gt, 0)
        def mml(q):
            for dc in range(8):
                r = q.matmul(out=ps[0][:, :], lhsT=Xt[:, dc * 128:(dc + 1) * 128],
                             rhs=wdkvb[:, dc * 512:(dc + 1) * 512],
                             start=(dc == 0), stop=(dc == 7))
            return r
        c.op("tensor", mml, ins=[Xt, wdkvb], outs=[ps[0]])
        c.op("vector", lambda q: q.tensor_copy(out=latf[:, :], in_=ps[0][:, :]),
             ins=[ps[0]], outs=[latf])
        layernorm_tile(c, latf[:, :], latn[:, :], latf, latn, stat, sq2, width=512)
        c.op("vector", lambda q: q.tensor_tensor(out=latn[:, :], in0=latn[:, :],
             in1=bb[:, 2048:2560], op=ALU.mult), ins=[latn, bb], outs=[latn])
        c.op("vector", lambda q: q.tensor_tensor(out=latn[:, :], in0=latn[:, :],
             in1=bb[:, 2560:3072], op=ALU.add), ins=[latn, bb], outs=[latn])
        def mk_trl(q):
            for k in range(4):
                r = q.transpose(out=ps[3][:, k * 128:(k + 1) * 128],
                                in_=latn[:, k * 128:(k + 1) * 128],
                                identity=c.s_ident[:, :])
            return r
        c.op("tensor", mk_trl, ins=[latn, c.s_ident], outs=[ps[3]])
        c.op("scalar", lambda q: q.copy(out=latT[:, :], in_=ps[3][:, 0:512]),
             ins=[ps[3]], outs=[latT])
        for mc in range(2):
            def mmv(q, mc=mc):
                for lc in range(4):
                    r = q.matmul(out=ps[mc][:, :], lhsT=latT[:, lc * 128:(lc + 1) * 128],
                                 rhs=wuvb[:, lc * 1024 + mc * 512:lc * 1024 + (mc + 1) * 512],
                                 start=(lc == 0), stop=(lc == 3))
                return r
            c.op("tensor", mmv, ins=[latT, wuvb], outs=[ps[mc]])
            c.op("scalar", lambda q, mc=mc: q.copy(
                out=vstg[:, mc * 512:(mc + 1) * 512], in_=ps[mc][:, :]),
                ins=[ps[mc]], outs=[vstg])
        c.op("gpsimd", lambda q, gt=gt: q.dma_start(
            out=vscr_h[gt * 128:(gt + 1) * 128, :], in_=vstg[:, :]),
            ins=[vstg], outs=[vscr_h], dma=True)
        for mc in range(2):
            def mmk(q, mc=mc):
                for lc in range(4):
                    r = q.matmul(out=ps[mc][:, :], lhsT=latT[:, lc * 128:(lc + 1) * 128],
                                 rhs=wukb[:, lc * 1024 + mc * 512:lc * 1024 + (mc + 1) * 512],
                                 start=(lc == 0), stop=(lc == 3))
                return r
            c.op("tensor", mmk, ins=[latT, wukb], outs=[ps[mc]])
            c.op("vector", lambda q, mc=mc: q.tensor_copy(
                out=prod[:, mc * 512:(mc + 1) * 512], in_=ps[mc][:, :]),
                ins=[ps[mc]], outs=[prod])
        rope(coskv_h, sinkv_h, gt * 128, prod[:, :], prod, rmsf[:, :], rmsf)
        for dgrp in range(2):
            def mk_trk(q, dgrp=dgrp):
                for k in range(4):
                    r = q.transpose(out=ps[3][:, k * 128:(k + 1) * 128],
                                    in_=rmsf[:, (4 * dgrp + k) * 128:(4 * dgrp + k + 1) * 128],
                                    identity=c.s_ident[:, :])
                return r
            c.op("tensor", mk_trk, ins=[rmsf, c.s_ident], outs=[ps[3]])
            c.op("scalar", lambda q, dgrp=dgrp, gt=gt: q.copy(
                out=kT.rearrange("p (dc t) -> p dc t", t=4096)[
                    :, 4 * dgrp:4 * dgrp + 4, gt * 128:(gt + 1) * 128],
                in_=ps[3][:, 0:512].rearrange("p (c t) -> p c t", t=128)),
                ins=[ps[3]], outs=[kT])

    psc = [c.ps("pc%d" % i, [128, 64], F32) for i in range(2)]
    psd = [c.ps("pd%d" % i, [128, 1], F32) for i in range(2)]
    load_w(WoT_h, 0)
    for qc in range(8):
        for kt in range(32):
            c.op("vector", lambda q, qc=qc, kt=kt: q.tensor_scalar(
                out=smalls[:, 5:6], in0=smalls[:, 4:5],
                scalar1=float(kt * 128 - qc * 256), scalar2=None, op0=ALU.add),
                ins=[smalls], outs=[smalls])
            c.op("vector", lambda q, kt=kt: q.tensor_scalar(
                out=mskc[:, kt * 256:(kt + 1) * 256], in0=c.s_iota[:, 0:256],
                scalar1=smalls[:, 5:6], scalar2=None, op0=ALU.is_ge),
                ins=[c.s_iota, smalls], outs=[mskc])
        for h in range(16):
            qsl = qslE if h % 2 == 0 else qslO
            par = h % 2
            c.op("sync", lambda q, qsl=qsl, h=h, qc=qc, par=par: q.dma_start(
                out=qsl[par * 64:(par + 1) * 64, :],
                in_=qscr_h[h * 64:(h + 1) * 64, qc * 256:(qc + 1) * 256]),
                ins=[qscr_h], outs=[qsl], dma=True)
            for kt in range(32):
                vt = Vt2[kt % 2]
                sps = ps[kt % 2]
                pf = Pf if kt % 2 == 0 else PfB
                pb = Pb if kt % 2 == 0 else PbB
                c.op("sync", lambda q, vt=vt, kt=kt, h=h: q.dma_start(
                    out=vt[:, :],
                    in_=vscr_h[kt * 128:(kt + 1) * 128, h * 64:(h + 1) * 64]),
                    ins=[vscr_h], outs=[vt], dma=True)
                def mmS(q, h=h, kt=kt, qsl=qsl, sps=sps):
                    return q.matmul(out=sps[:, 0:256],
                                    lhsT=kT[:, (h // 2) * 4096 + kt * 128:
                                            (h // 2) * 4096 + (kt + 1) * 128],
                                    rhs=qsl[:, :],
                                    start=True, stop=True)
                c.op("tensor", mmS, ins=[kT, qsl], outs=[sps])
                c.op("scalar", lambda q, sps=sps, pf=pf: q.activation(
                     out=pf[:, :], in_=sps[:, 0:256],
                     func=AF.Exp, scale=0.125), ins=[sps], outs=[pf])
                c.op("vector", lambda q, kt=kt, pf=pf, pb=pb: q.tensor_tensor(
                    out=pb[:, :], in0=pf[:, :],
                    in1=mskc[:, kt * 256:(kt + 1) * 256],
                    op=ALU.mult), ins=[pf, mskc], outs=[pb])
                def mmcd(q, kt=kt, vt=vt, pb=pb):
                    for qs in range(2):
                        q.matmul(out=psc[qs][:, :],
                                 lhsT=pb[:, qs * 128:(qs + 1) * 128],
                                 rhs=vt[:, :], start=(kt == 0), stop=(kt == 31))
                        r = q.matmul(out=psd[qs][:, :],
                                     lhsT=pb[:, qs * 128:(qs + 1) * 128],
                                     rhs=s_ones128[:, 0:1],
                                     start=(kt == 0), stop=(kt == 31))
                    return r
                c.op("tensor", mmcd, ins=[pb, vt, s_ones128],
                     outs=[psc[0], psc[1], psd[0], psd[1]])
            for qs in range(2):
                c.op("vector", lambda q, qs=qs: q.reciprocal(
                    out=smalls[:, 6 + qs:7 + qs], in_=psd[qs][:, :]),
                    ins=[psd[qs]], outs=[smalls])
                dstb = ctxA if qs == 0 else ctxB
                c.op("vector", lambda q, dstb=dstb, h=h, qs=qs: q.tensor_scalar(
                    out=dstb[:, h * 64:(h + 1) * 64], in0=psc[qs][:, :],
                    scalar1=smalls[:, 6 + qs:7 + qs], scalar2=None, op0=ALU.mult),
                    ins=[psc[qs], smalls], outs=[dstb])
        for qs in range(2):
            srcb = ctxA if qs == 0 else ctxB
            for dgrp in range(2):
                def mk_tr3(q, dgrp=dgrp, srcb=srcb):
                    for k in range(4):
                        r = q.transpose(out=ps[3][:, k * 128:(k + 1) * 128],
                                        in_=srcb[:, (4 * dgrp + k) * 128:(4 * dgrp + k + 1) * 128],
                                        identity=c.s_ident[:, :])
                    return r
                c.op("tensor", mk_tr3, ins=[srcb, c.s_ident], outs=[ps[3]])
                c.op("scalar", lambda q, dgrp=dgrp: q.copy(
                    out=ctxT[:, dgrp * 512:(dgrp + 1) * 512], in_=ps[3][:, 0:512]),
                    ins=[ps[3]], outs=[ctxT])
            tg = qc * 256 + qs * 128
            c.op("sync", lambda q, tg=tg: q.dma_start(out=sx[:, :],
                 in_=x_h[tg:tg + 128, :]), outs=[sx], dma=True)
            layernorm_tile(c, sx[:, :], xn[:, :], sx, xn, stat, sq)
            for mc in range(2):
                pp = ps[mc]
                def mmo(q, mc=mc, pp=pp):
                    for dc in range(8):
                        wsb = wsbA if dc < 4 else wsbB
                        dcl = dc % 4
                        r = q.matmul(out=pp[:, :], lhsT=ctxT[:, dc * 128:(dc + 1) * 128],
                                     rhs=wsb[:, dcl * 1024 + mc * 512:dcl * 1024 + (mc + 1) * 512],
                                     start=(dc == 0), stop=(dc == 7))
                    return r
                c.op("tensor", mmo, ins=[ctxT, wsbA, wsbB], outs=[pp])
                c.op("vector", lambda q, mc=mc, pp=pp: q.tensor_tensor(
                    out=yb[:, mc * 512:(mc + 1) * 512], in0=pp[:, :],
                    in1=xn[:, mc * 512:(mc + 1) * 512], op=ALU.add),
                    ins=[pp, xn], outs=[yb])
            c.op("gpsimd", lambda q, tg=tg: q.dma_start(
                out=x1scr_h[tg:tg + 128, :], in_=yb[:, :]),
                ins=[yb], outs=[x1scr_h], dma=True)

    c.barrier()
    c.free(kT, wsbB, wsbA, wuvb, wukb, wdkvb, mskc)
    for h in (psd[1], psd[0], psc[1], psc[0]):
        c.cms.pop(id(h)).__exit__(None, None, None)
    psy = [c.ps("py%d" % i, [128, 512], F32) for i in range(4)]
    sx1 = c.sb("sx1", [128, 4 * 1024], F32)
    Xt2 = c.sb("Xt2", [128, 8 * 512], BF16)
    Xt2l = c.sb("Xt2l", [128, 8 * 512], BF16)
    gwbl = c.sb("gwbl", [128, 8 * 4], BF16)
    gstg = c.sb("gstg", [128, 512], BF16)
    heT = c.sb("heT", [128, 32 * 512], BF16)
    selb = c.sb("selb", [128, 16], F32)
    ymoe = c.sb("ymoe", [128, 4 * 1024], F32)
    gwb = c.sb("gwb", [128, 8 * 4], BF16)
    b1c = c.sb("b1c", [128, 4 * 32], F32)
    b2bc = c.sb("b2bc", [128, 1024], F32)
    wb2 = c.sb("wb2", [128, 4096], BF16)

    for dc in range(8):
        c.op("sync", lambda q, dc=dc: q.dma_start(out=wstage[:, 0:4],
             in_=gwT_h[dc * 128:(dc + 1) * 128, :]), outs=[wstage], dma=True)
        c.op("scalar", lambda q, dc=dc: q.copy(out=gwb[:, dc * 4:(dc + 1) * 4],
             in_=wstage[:, 0:4]), ins=[wstage], outs=[gwb])
        c.op("vector", lambda q, dc=dc: q.tensor_tensor(
            out=gwbl[:, dc * 4:(dc + 1) * 4], in0=wstage[:, 0:4],
            in1=gwb[:, dc * 4:(dc + 1) * 4], op=ALU.subtract),
            ins=[wstage, gwb], outs=[gwbl])
    for e in range(4):
        c.op("sync", lambda q, e=e: q.dma_start(out=wstage[0:1, 0:512],
             in_=b1_h[e:e + 1, 0:512]), outs=[wstage], dma=True)
        for part in range(8):
            if part > 0:
                c.op("sync", lambda q, e=e, part=part: q.dma_start(
                    out=wstage[0:1, 0:512],
                    in_=b1_h[e:e + 1, part * 512:(part + 1) * 512]),
                    outs=[wstage], dma=True)
            def mkb(q, e=e, part=part):
                for i in range(4):
                    r = q.matmul(out=ps[1][:, part * 4 + i:part * 4 + i + 1],
                                 lhsT=wstage[0:1, i * 128:(i + 1) * 128],
                                 rhs=c.s_ones[0:1, 0:1], start=True, stop=True)
                return r
            c.op("tensor", mkb, ins=[wstage, c.s_ones], outs=[ps[1]])
        c.op("vector", lambda q, e=e: q.tensor_copy(out=b1c[:, e * 32:(e + 1) * 32],
             in_=ps[1][:, 0:32]), ins=[ps[1]], outs=[b1c])

    for tc in range(4):
        c.op("sync", lambda q, tc=tc: q.dma_start(out=sx1[:, :].rearrange(
             "p (c m) -> p c m", m=1024),
             in_=x1scr_h.rearrange("(c p) m -> p c m", p=128)[:, tc * 4:(tc + 1) * 4, :]),
             ins=[x1scr_h], outs=[sx1], dma=True)
        c.op("vector", lambda q: q.memset(ymoe[:, :], 0.0), outs=[ymoe])
        for tt in range(4):
            layernorm_tile(c, sx1[:, tt * 1024:(tt + 1) * 1024], rmsf[:, :],
                           sx1, rmsf, stat, sq, eps=1e-6, rms=True)
            c.op("vector", lambda q: q.tensor_tensor(out=rmsf[:, :], in0=rmsf[:, :],
                 in1=bb[:, 1024:2048], op=ALU.mult), ins=[rmsf, bb], outs=[rmsf])
            for dgrp in range(2):
                def mk_tr4(q, dgrp=dgrp):
                    for k in range(4):
                        r = q.transpose(out=ps[3][:, k * 128:(k + 1) * 128],
                                        in_=rmsf[:, (4 * dgrp + k) * 128:(4 * dgrp + k + 1) * 128],
                                        identity=c.s_ident[:, :])
                    return r
                c.op("tensor", mk_tr4, ins=[rmsf, c.s_ident], outs=[ps[3]])
                c.op("scalar", lambda q, dgrp=dgrp, tt=tt: q.copy(
                    out=Xt2.rearrange("p (dc t) -> p dc t", t=512)[
                        :, 4 * dgrp:4 * dgrp + 4, tt * 128:(tt + 1) * 128],
                    in_=ps[3][:, 0:512].rearrange("p (c t) -> p c t", t=128)),
                    ins=[ps[3]], outs=[Xt2])
                c.op("scalar", lambda q, dgrp=dgrp, tt=tt: q.copy(
                    out=gstg[:, :],
                    in_=Xt2.rearrange("p (dc t) -> p dc t", t=512)[
                        :, 4 * dgrp:4 * dgrp + 4, tt * 128:(tt + 1) * 128]),
                    ins=[Xt2], outs=[gstg])
                c.op("vector", lambda q, dgrp=dgrp, tt=tt: q.tensor_tensor(
                    out=Xt2l.rearrange("p (dc t) -> p dc t", t=512)[
                        :, 4 * dgrp:4 * dgrp + 4, tt * 128:(tt + 1) * 128],
                    in0=ps[3][:, 0:512].rearrange("p (c t) -> p c t", t=128),
                    in1=gstg[:, :].rearrange("p (c t) -> p c t", t=128),
                    op=ALU.subtract), ins=[ps[3], gstg], outs=[Xt2l])
            def mmg(q, tt=tt):
                for dc in range(8):
                    q.matmul(out=ps[0][:, 0:4],
                             lhsT=Xt2[:, dc * 512 + tt * 128:dc * 512 + (tt + 1) * 128],
                             rhs=gwb[:, dc * 4:(dc + 1) * 4],
                             start=(dc == 0), stop=False)
                    q.matmul(out=ps[0][:, 0:4],
                             lhsT=Xt2l[:, dc * 512 + tt * 128:dc * 512 + (tt + 1) * 128],
                             rhs=gwb[:, dc * 4:(dc + 1) * 4],
                             start=False, stop=False)
                    r = q.matmul(out=ps[0][:, 0:4],
                                 lhsT=Xt2[:, dc * 512 + tt * 128:dc * 512 + (tt + 1) * 128],
                                 rhs=gwbl[:, dc * 4:(dc + 1) * 4],
                                 start=False, stop=(dc == 7))
                return r
            c.op("tensor", mmg, ins=[Xt2, Xt2l, gwb, gwbl], outs=[ps[0]])
            c.op("vector", lambda q: q.tensor_tensor(out=r1[:, 0:4], in0=ps[0][:, 0:4],
                 in1=bb[:, 3072:3076], op=ALU.add), ins=[ps[0], bb], outs=[r1])
            c.op("vector", lambda q: q.tensor_reduce(out=r1[:, 4:5], in_=r1[:, 0:4],
                 axis=AX.X, op=ALU.max), ins=[r1], outs=[r1])
            c.op("vector", lambda q: q.tensor_tensor(out=r2[:, 0:1], in0=r1[:, 0:1],
                 in1=r1[:, 4:5], op=ALU.is_ge), ins=[r1], outs=[r2])
            c.op("vector", lambda q: q.tensor_tensor(out=r2[:, 4:5], in0=r1[:, 1:2],
                 in1=r1[:, 4:5], op=ALU.is_ge), ins=[r1], outs=[r2])
            c.op("vector", lambda q: q.tensor_scalar(out=r2[:, 5:6], in0=r2[:, 0:1],
                 scalar1=-1.0, scalar2=1.0, op0=ALU.mult, op1=ALU.add),
                 ins=[r2], outs=[r2])
            c.op("vector", lambda q: q.tensor_tensor(out=r2[:, 1:2], in0=r2[:, 4:5],
                 in1=r2[:, 5:6], op=ALU.mult), ins=[r2], outs=[r2])
            c.op("vector", lambda q: q.tensor_tensor(out=r2[:, 4:5], in0=r1[:, 2:3],
                 in1=r1[:, 4:5], op=ALU.is_ge), ins=[r1, r2], outs=[r2])
            c.op("vector", lambda q: q.tensor_tensor(out=r2[:, 5:6], in0=r2[:, 0:1],
                 in1=r2[:, 1:2], op=ALU.add), ins=[r2], outs=[r2])
            c.op("vector", lambda q: q.tensor_scalar(out=r2[:, 5:6], in0=r2[:, 5:6],
                 scalar1=-1.0, scalar2=1.0, op0=ALU.mult, op1=ALU.add),
                 ins=[r2], outs=[r2])
            c.op("vector", lambda q: q.tensor_tensor(out=r2[:, 2:3], in0=r2[:, 4:5],
                 in1=r2[:, 5:6], op=ALU.mult), ins=[r2], outs=[r2])
            c.op("vector", lambda q: q.tensor_tensor(out=r2[:, 4:5], in0=r2[:, 0:1],
                 in1=r2[:, 1:2], op=ALU.add), ins=[r2], outs=[r2])
            c.op("vector", lambda q: q.tensor_tensor(out=r2[:, 4:5], in0=r2[:, 4:5],
                 in1=r2[:, 2:3], op=ALU.add), ins=[r2], outs=[r2])
            c.op("vector", lambda q: q.tensor_scalar(out=r2[:, 3:4], in0=r2[:, 4:5],
                 scalar1=-1.0, scalar2=1.0, op0=ALU.mult, op1=ALU.add),
                 ins=[r2], outs=[r2])
            c.op("vector", lambda q, tt=tt: q.tensor_copy(out=selb[:, tt * 4:(tt + 1) * 4],
                 in_=r2[:, 0:4]), ins=[r2], outs=[selb])
        for e in range(4):
            for mt in range(32):
                for half in range(2):
                    c.op("sync", lambda q, e=e, mt=mt, half=half: q.dma_start(
                        out=wstage[:, :].rearrange("p (c m) -> p c m", m=128),
                        in_=W1T_h.rearrange("e (c p) m -> e p c m", p=128)[
                            e, :, half * 4:(half + 1) * 4, mt * 128:(mt + 1) * 128]),
                        outs=[wstage], dma=True)
                    c.op("scalar", lambda q, half=half: q.copy(
                        out=wb2[:, half * 512:(half + 1) * 512], in_=wstage[:, :]),
                        ins=[wstage], outs=[wb2])
                def mmh(q, mt=mt):
                    for dc in range(8):
                        r = q.matmul(out=ps[0][:, :],
                                     lhsT=wb2[:, dc * 128:(dc + 1) * 128],
                                     rhs=Xt2[:, dc * 512:(dc + 1) * 512],
                                     start=(dc == 0), stop=(dc == 7))
                    return r
                c.op("tensor", mmh, ins=[wb2, Xt2], outs=[ps[0]])
                c.op("scalar", lambda q, e=e, mt=mt: q.activation(
                    out=heT[:, mt * 512:(mt + 1) * 512], in_=ps[0][:, :],
                    func=AF.Relu, bias=b1c[:, e * 32 + mt:e * 32 + mt + 1]),
                    ins=[ps[0], b1c], outs=[heT])
            c.op("sync", lambda q, e=e: q.dma_start(out=wstage[0:1, 0:512],
                 in_=b2_h[e:e + 1, 0:512]), outs=[wstage], dma=True)
            row_bcast(c, wstage[0:1, 0:512], wstage, b2bc[:, 0:512], b2bc,
                      ps[0], 512)
            c.op("sync", lambda q, e=e: q.dma_start(out=wstage[0:1, 0:512],
                 in_=b2_h[e:e + 1, 512:1024]), outs=[wstage], dma=True)
            row_bcast(c, wstage[0:1, 0:512], wstage, b2bc[:, 512:1024], b2bc,
                      ps[0], 512)
            allps = psy + ps
            for mt in range(32):
                c.op("sync", lambda q, e=e, mt=mt: q.dma_start(
                    out=wstage[:, 0:512],
                    in_=W2T_h[e, mt * 128:(mt + 1) * 128, 0:512]),
                    outs=[wstage], dma=True)
                c.op("scalar", lambda q: q.copy(out=wb2[:, 0:512],
                     in_=wstage[:, 0:512]), ins=[wstage], outs=[wb2])
                c.op("sync", lambda q, e=e, mt=mt: q.dma_start(
                    out=wstage[:, 0:512],
                    in_=W2T_h[e, mt * 128:(mt + 1) * 128, 512:1024]),
                    outs=[wstage], dma=True)
                c.op("scalar", lambda q: q.copy(out=wb2[:, 512:1024],
                     in_=wstage[:, 0:512]), ins=[wstage], outs=[wb2])
                def mmy(q, mt=mt):
                    for tt in range(4):
                        for mc in range(2):
                            r = q.matmul(out=allps[tt * 2 + mc][:, :],
                                         lhsT=heT[:, mt * 512 + tt * 128:mt * 512 + (tt + 1) * 128],
                                         rhs=wb2[:, mc * 512:(mc + 1) * 512],
                                         start=(mt == 0), stop=(mt == 31))
                    return r
                c.op("tensor", mmy, ins=[heT, wb2], outs=allps)
            for tt in range(4):
                for mc in range(2):
                    pp = allps[tt * 2 + mc]
                    c.op("vector", lambda q, pp=pp, mc=mc: q.tensor_tensor(
                        out=sq2[:, :], in0=pp[:, :],
                        in1=b2bc[:, mc * 512:(mc + 1) * 512], op=ALU.add),
                        ins=[pp, b2bc], outs=[sq2])
                    c.op("vector", lambda q, tt=tt, mc=mc, e=e: q.scalar_tensor_tensor(
                        out=ymoe[:, tt * 1024 + mc * 512:tt * 1024 + (mc + 1) * 512],
                        in0=sq2[:, :], scalar=selb[:, tt * 4 + e:tt * 4 + e + 1],
                        in1=ymoe[:, tt * 1024 + mc * 512:tt * 1024 + (mc + 1) * 512],
                        op0=ALU.mult, op1=ALU.add),
                        ins=[sq2, selb, ymoe], outs=[ymoe])
        for tt in range(4):
            tg = tc * 512 + tt * 128
            c.op("sync", lambda q, tg=tg: q.dma_start(out=sx[:, :],
                 in_=x_h[tg:tg + 128, :]), outs=[sx], dma=True)
            c.op("vector", lambda q, tt=tt: q.tensor_tensor(out=yb[:, :],
                 in0=sx1[:, tt * 1024:(tt + 1) * 1024],
                 in1=ymoe[:, tt * 1024:(tt + 1) * 1024], op=ALU.add),
                 ins=[sx1, ymoe], outs=[yb])
            c.op("vector", lambda q: q.tensor_tensor(out=yb[:, :], in0=yb[:, :],
                 in1=sx[:, :], op=ALU.add), ins=[yb, sx], outs=[yb])
            c.op("gpsimd", lambda q, tg=tg: q.dma_start(
                out=y_h[tg:tg + 128, :], in_=yb[:, :]), ins=[yb], dma=True)
    c.finish()
    return c.nc


_L0FFN_NC = None


def l0ffn_apply(x_full, weights):
    global _L0FFN_NC
    if _L0FFN_NC is None:
        _L0FFN_NC = build_l0ffn()
    (norm1_w, norm2_w, Wq, Wdkv, Wuk, Wuv, Wo, ln_g, ln_b,
     gate_w, gate_b, W1, b1, W2, b2) = weights
    f = np.ascontiguousarray
    half_d = 32
    pos = np.arange(S, dtype=np.float32)[:, None]
    inv_freq = 1.0 / 10000.0 ** (np.arange(half_d, dtype=np.float32) / half_d)
    ang = pos * inv_freq[None, :]
    cosf = np.tile(np.cos(ang), (1, 16)).astype(np.float32)
    sinf = np.tile(np.sin(ang), (1, 16)).astype(np.float32)
    shared = {
        "WqT": f(Wq.T), "WdkvT": f(Wdkv.T), "WukT": f(Wuk.T), "WuvT": f(Wuv.T),
        "WoT": f(Wo.T), "n1": f(norm1_w.reshape(1, -1)), "n2": f(norm2_w.reshape(1, -1)),
        "lng": f(ln_g.reshape(1, -1)), "lnb": f(ln_b.reshape(1, -1)),
        "gwT": f(gate_w.T), "gb": f(gate_b.reshape(1, -1)),
        "W1T": f(W1.transpose(0, 2, 1)), "b1": f(b1),
        "W2T": f(W2.transpose(0, 2, 1)), "b2": f(b2),
        "coskv": cosf, "sinkv": sinf,
    }
    per_core = []
    for cid in range(NCORES):
        be, half = cid // 2, cid % 2
        per_core.append({
            "x": f(x_full[be, half * 2048:(half + 1) * 2048]),
            "xkv": f(x_full[be]),
            "qoff": (np.arange(128, dtype=np.float32)[:, None] - half * 2048),
            "cosq": f(cosf[half * 2048:(half + 1) * 2048]),
            "sinq": f(sinf[half * 2048:(half + 1) * 2048]),
        })
    rs = run_launch(_L0FFN_NC, shared, per_core, ["y"])
    y = np.empty((B, S, D), np.float32)
    for cid in range(NCORES):
        be, half = cid // 2, cid % 2
        y[be, half * 2048:(half + 1) * 2048] = rs[cid]["y"]
    return y



def kernel(**inp):
    x = np.asarray(inp["L0_attn_a_x"], np.float32)
    x = l0attna_apply(x, [inp["L0_attn_a_Wqk"], inp["L0_attn_a_bqk"],
                          inp["L0_attn_a_Wv"], inp["L0_attn_a_bv"],
                          inp["L0_attn_a_Wn"], inp["L0_attn_a_bn"],
                          inp["L0_attn_a_Wo"], inp["L0_attn_a_bo"]])
    l0ffn_w = [inp["L0_ffn_norm1_w"], inp["L0_ffn_norm2_w"], inp["L0_ffn_Wq"],
               inp["L0_ffn_Wdkv"], inp["L0_ffn_Wuk"], inp["L0_ffn_Wuv"],
               inp["L0_ffn_Wo"], inp["L0_ffn_ln_g"], inp["L0_ffn_ln_b"],
               inp["L0_ffn_gate_w"], inp["L0_ffn_gate_b"], inp["L0_ffn_W1"],
               inp["L0_ffn_b1"], inp["L0_ffn_W2"], inp["L0_ffn_b2"]]
    x = l0ffn_apply(x, l0ffn_w)
    x = l0attnb_apply(x, [inp["L0_attn_b_Wq"], inp["L0_attn_b_Wkv"],
                          inp["L0_attn_b_Wo"]])
    x = l0ffn_apply(x, l0ffn_w)
    x = l1attna_apply(x, [inp["L1_attn_a_WQ"], inp["L1_attn_a_WK"],
                          inp["L1_attn_a_WV"]])
    l1ffn_w = [inp["L1_ffn_w1_w"], inp["L1_ffn_w1_s"], inp["L1_ffn_w2_w"],
               inp["L1_ffn_w2_s"], inp["L1_ffn_h1_w"], inp["L1_ffn_h1_s"],
               inp["L1_ffn_h2_w"], inp["L1_ffn_h2_s"], inp["L1_ffn_w3_w"],
               inp["L1_ffn_w3_s"]]
    x = l1ffn_apply(x, l1ffn_w)
    x = l1attnb_apply(x, [inp["L1_attn_b_Bmat"], inp["L1_attn_b_Cmat"],
                          inp["L1_attn_b_Dvec"], inp["L1_attn_b_params_log"],
                          inp["L1_attn_b_ln_w"], inp["L1_attn_b_ln_b"]])
    x = l1ffn_apply(x, l1ffn_w)
    return x
